# version 04
# baseline (speedup 1.0000x reference)
import numpy as np
import concourse.bass as bass
import concourse.mybir as mybir
from concourse.bass_utils import run_bass_kernel_spmd
from contextlib import ExitStack

F32 = mybir.dt.float32
F16 = mybir.dt.float16
BF16 = mybir.dt.bfloat16
I32 = mybir.dt.int32
AF = mybir.ActivationFunctionType
ALU = mybir.AluOpType
AX = mybir.AxisListType

B, T, D = 32, 2048, 1024
H, KVH, HD = 16, 4, 64
RANK = 32
NE, FFN = 8, 2048
NSEQ, NT = 4, 16
SLAB = 256
TPS = SLAB // 128
NSLAB = T // SLAB
NCH = D // 128
QMAXF = 31.0
INV31 = float(np.float32(1.0) / np.float32(31.0))
SOFT_SCALE = 0.125
MASKVAL = -1e31


class Prog:
    ENGINES = ("sync", "gpsimd", "vector", "scalar", "tensor")
    DMAQ = {"sync": "dsync", "gpsimd": "dgp"}
    NPOOL = 4

    def __init__(self):
        self.ops = []
        self.writer = {}
        self.readers = {}
        self.bar = []

    def op(self, eng, emit, r=(), w=(), sr=(), dma=False, tag=""):
        deps = {}
        for b in r:
            j = self.writer.get(b)
            if j is not None:
                deps.setdefault(j, False)
        for b in sr:
            j = self.writer.get(b)
            if j is not None:
                deps[j] = True
        for b in w:
            j = self.writer.get(b)
            if j is not None:
                deps.setdefault(j, False)
            for jr in self.readers.get(b, ()):
                deps.setdefault(jr, False)
        for j in self.bar:
            deps.setdefault(j, False)
        i = len(self.ops)
        self.ops.append(dict(eng=eng, emit=emit, deps=deps, dma=dma, tag=tag,
                             milestone=False, sem=None, val=None, inc=0))
        for b in r:
            self.readers.setdefault(b, []).append(i)
        for b in sr:
            self.readers.setdefault(b, []).append(i)
        for b in w:
            self.writer[b] = i
            self.readers[b] = []
        return i

    def barrier(self):
        last = {}
        for i, o in enumerate(self.ops):
            last[o["eng"]] = i
        self.bar = list(last.values())

    def _wait_needed(self, i, j, scalar):
        return True

    def emit(self, nc, es, block):
        for i, o in enumerate(self.ops):
            for j, scalar in o["deps"].items():
                if self._wait_needed(i, j, scalar):
                    self.ops[j]["milestone"] = True
        for o in self.ops:
            if o["dma"]:
                o["milestone"] = True
        sems = {e: es.enter_context(nc.semaphore(f"se_{e}")) for e in self.ENGINES}
        dsems = {q: [es.enter_context(nc.semaphore(f"sd_{q}{k}"))
                     for k in range(self.NPOOL)] for q in self.DMAQ.values()}
        nc.all_engine_barrier()
        for s in list(sems.values()) + [s for p in dsems.values() for s in p]:
            nc.gpsimd.sem_clear(s)
        nc.all_engine_barrier()
        cnt = {e: 0 for e in self.ENGINES}
        dcnt = {q: 0 for q in self.DMAQ.values()}
        for o in self.ops:
            if not o["milestone"]:
                continue
            if o["dma"]:
                q = self.DMAQ[o["eng"]]
                k = dcnt[q]
                dcnt[q] += 1
                o["sem"] = dsems[q][k % self.NPOOL]
                o["val"] = 16 * (k // self.NPOOL + 1)
                o["inc"] = 16
                if k >= self.NPOOL:
                    o["self_wait"] = (o["sem"], 16 * (k // self.NPOOL))
            else:
                cnt[o["eng"]] += 1
                o["sem"], o["val"], o["inc"] = sems[o["eng"]], cnt[o["eng"]], 1
        marks = {e: {} for e in self.ENGINES}
        for i, o in enumerate(self.ops):
            waits = {}
            for j, scalar in o["deps"].items():
                if not self._wait_needed(i, j, scalar):
                    continue
                oj = self.ops[j]
                key = id(oj["sem"])
                if waits.get(key, (None, -1))[1] < oj["val"]:
                    waits[key] = (oj["sem"], oj["val"])
            if o.get("self_wait"):
                s, v = o["self_wait"]
                if waits.get(id(s), (None, -1))[1] < v:
                    waits[id(s)] = (s, v)
            m = marks[o["eng"]]
            o["waits"] = []
            for key, (s, v) in waits.items():
                if m.get(key, -1) >= v:
                    continue
                m[key] = v
                o["waits"].append((s, v))
        streams = {e: [o for o in self.ops if o["eng"] == e] for e in self.ENGINES}

        def mk(ops, is_sync):
            def body(eng):
                for o in ops:
                    for (s, v) in o["waits"]:
                        eng.wait_ge(s, v)
                    inst = o["emit"](eng)
                    if o["milestone"]:
                        assert inst is not None, o["tag"]
                        inst.then_inc(o["sem"], o["inc"])
                if is_sync:
                    for q, pool in dsems.items():
                        n = dcnt[q]
                        for k in range(min(self.NPOOL, n)):
                            eng.wait_ge(pool[k], 16 * ((n - k - 1) // self.NPOOL + 1))
                    for e in self.ENGINES:
                        if e != "sync" and cnt[e] > 0:
                            eng.wait_ge(sems[e], cnt[e])
            return body

        for e in self.ENGINES:
            if streams[e] or e == "sync":
                getattr(block, e)(mk(streams[e], e == "sync"))
        return {e: len(v) for e, v in streams.items()}


class KB:
    CV = dict(qfm=0, yfm=1024, P=2048, Pt=3072, rsA=5120, rsB=6144,
              qtm=7168, ltm=8192, ktm=9216, lfm=9472, WT=10496)
    MCV = dict(hfm=0, gbuf=8192, h2f=12288, htm=13312, hfmf=13824, res=14848, x1t=12288)
    FQ = dict(wt=0, y=1024, q0f=2048, t1=3072, q1=4096, q2=5120,
              d1=6144, d2=7168, qi=8192, asm=9216, wdq=13500)

    def __init__(self, debug_taps=()):
        self.debug_taps = dict(debug_taps)

    def sslot(self, n):
        c = self._sc
        if c + n > 64:
            c = 0
        self._sc = c + n
        return c

    def skey(self, c, i=0):
        return f"scal:{c + i}"

    def sap(self, c, i=0):
        return self.S["scal"][:, c + i:c + i + 1]

    def ld(self, dst, src, wkeys, rkeys=(), q="sync", tag="ld"):
        self.P.op(q, (lambda eng, d=dst, s=src: eng.dma_start(out=d, in_=s)),
                  r=tuple(rkeys), w=tuple(wkeys), dma=True, tag=tag)

    def st(self, dst, src, rkeys, wkeys=(), q="sync", tag="st"):
        self.P.op(q, (lambda eng, d=dst, s=src: eng.dma_start(out=d, in_=s)),
                  r=tuple(rkeys), w=tuple(wkeys), dma=True, tag=tag)

    def v(self, emit, r=(), w=(), sr=(), tag=""):
        self.P.op("vector", emit, r=r, w=w, sr=sr, tag=tag)

    def a(self, emit, r=(), w=(), sr=(), tag=""):
        self.P.op("scalar", emit, r=r, w=w, sr=sr, tag=tag)

    def t(self, emit, r=(), w=(), sr=(), tag=""):
        self.P.op("tensor", emit, r=r, w=w, sr=sr, tag=tag)

    def mmg(self, psum_ap, pairs, rkeys, wkeys, tag="mm"):
        def em(eng, p=psum_ap, prs=tuple(pairs)):
            inst = None
            n = len(prs)
            for i, (l, rr) in enumerate(prs):
                inst = eng.matmul(out=p, lhsT=l, rhs=rr, start=(i == 0),
                                  stop=(i == n - 1))
            return inst
        self.t(em, r=tuple(rkeys), w=tuple(wkeys), tag=tag)

    def cvf(self, nm, n):
        return self.ovl[:, self.CV[nm]:self.CV[nm] + n]

    def cvb(self, nm, n):
        return self.ovb[:, 2 * self.CV[nm]:2 * self.CV[nm] + n]

    def mvf(self, nm, n):
        return self.ovl[:, self.MCV[nm]:self.MCV[nm] + n]

    def mvb(self, nm, n):
        return self.ovb[:, 2 * self.MCV[nm]:2 * self.MCV[nm] + n]

    def stats(self, src_ap, rkeys, kind, tag="st"):
        S = self.S
        bslot = self._bn
        self._bn ^= 1
        bn = S["bn"][:, bslot * 12:(bslot + 1) * 12]
        bnk = f"bn:{bslot}"
        c = self.sslot(4)

        def e1(eng, a=src_ap, b=bn):
            eng.bn_stats(out=b[:, 0:6], in_=a[:, 0:512])
            return eng.bn_stats(out=b[:, 6:12], in_=a[:, 512:1024])
        self.v(e1, r=tuple(rkeys), w=(bnk,), tag=tag + "bn")

        def e2(eng, b=bn, o=S["scal"][:, c:c + 2]):
            return eng.bn_aggr(out=o, in_=b)
        self.v(e2, r=(bnk,), w=(self.skey(c), self.skey(c, 1)), tag=tag + "ag")
        mean, var = self.sap(c), self.sap(c, 1)
        if kind == "rms":
            o2 = self.sap(c, 2)

            def e3(eng, m=mean, o=o2):
                return eng.tensor_tensor(out=o, in0=m, in1=m, op=ALU.mult)
            self.v(e3, sr=(self.skey(c),), w=(self.skey(c, 2),), tag=tag + "m2")

            def e4(eng, vv=var, o=o2):
                return eng.tensor_tensor(out=o, in0=o, in1=vv, op=ALU.add)
            self.v(e4, sr=(self.skey(c, 1), self.skey(c, 2)), w=(self.skey(c, 2),),
                   tag=tag + "ms")
            vsrc, vk = o2, self.skey(c, 2)
        else:
            vsrc, vk = var, self.skey(c, 1)
        rstd = self.sap(c, 3)

        def e4e(eng, i=vsrc):
            return eng.tensor_single_scalar(out=i, in_=i, scalar=1e-5, op=ALU.add)
        self.v(e4e, sr=(vk,), w=(vk,), tag=tag + "ep")

        def e5(eng, i=vsrc, o=rstd):
            return eng.activation(out=o, in_=i, func=AF.Ln)
        self.a(e5, sr=(vk,), w=(self.skey(c, 3),), tag=tag + "ln")

        def e6(eng, o=rstd):
            return eng.activation(out=o, in_=o, func=AF.Exp, scale=-0.5)
        self.a(e6, sr=(self.skey(c, 3),), w=(self.skey(c, 3),), tag=tag + "ex")
        return self.skey(c), mean, self.skey(c, 3), rstd

    def tab(self, name, tile, reps):
        tt = self.S["s_" + name][:]
        return bass.AP(tt.tensor, tt.offset + tile * HD,
                       [[tt.ap[0][0], 128], [0, reps], [1, HD]])

    def tab4(self, name, tile, reps):
        tt = self.S["s_" + name][:]
        return bass.AP(tt.tensor, tt.offset + tile * HD,
                       [[tt.ap[0][0], 128], [0, reps], [32, 2], [1, 32]])

    def shufw(self, ap, nheads):
        return bass.AP(ap.tensor, ap.offset + 32,
                       [[ap.ap[0][0], 128], [64, nheads], [-32, 2], [1, 32]])

    def g64(self, ap, nheads):
        return bass.AP(ap.tensor, ap.offset,
                       [[ap.ap[0][0], 128], [64, nheads], [32, 2], [1, 32]])

    def build(self):
        nc = bass.Bass("TRN2", target_bir_lowering=False)
        self.nc = nc
        self.P = Prog()
        self._sc = 0
        self._bn = 0

        dram = {}
        specs = [
            ("x", [NSEQ * T, D], F32),
            ("Wq", [D, D], F32), ("Wk", [KVH * HD, D], F32),
            ("Wv", [KVH * HD, D], F32), ("Wo", [D, D], F32),
            ("qA", [D, RANK], F32), ("kA", [D, RANK], F32), ("vA", [D, RANK], F32),
            ("pA", [D, RANK], F32),
            ("qB", [RANK, D], F32), ("kB", [RANK, KVH * HD], F32),
            ("vB", [RANK, KVH * HD], F32), ("pB", [RANK, D], F32),
            ("q_gain", [1, D], F32), ("head_gate", [1, H], F32),
            ("ln1_w", [1, D], F32), ("ln2_w", [1, D], F32),
            ("wqkvT", [D, (H + 2 * KVH) * HD], F32), ("wo2T", [D, H * HD], F32),
            ("gateT", [D, NE], F32),
            ("w13T", [NE, D, 2 * FFN], F32), ("w2T", [NE, FFN, D], F32),
            ("cpA", [T, HD], F32), ("spA", [T, HD], F32),
            ("cpB", [T, HD], F32), ("spB", [T, HD], F32),
            ("maskadd", [128, 128], F32), ("identf", [128, 128], F32),
            ("onesrow", [1, 128], F32),
        ]
        for nm, shp, dt in specs:
            dram[nm] = nc.dram_tensor(nm, shp, dt, kind="ExternalInput").ap()
        dram["identb"] = nc.dram_tensor("identb", [128, 128], BF16,
                                        kind="ExternalInput").ap()
        dram["y"] = nc.dram_tensor("y", [NSEQ * T, D], F32, kind="ExternalOutput").ap()
        for nm, wid in [("wsq", D), ("wsk", KVH * HD), ("wsv", KVH * HD), ("wso", D)]:
            dram[nm] = nc.dram_tensor(nm, [D, wid], BF16, kind="Internal").ap()
        dram["x1scr"] = nc.dram_tensor("x1scr", [T, D], F32, kind="Internal").ap()
        taps = {}
        for nm, (shp, dt) in self.debug_taps.items():
            taps[nm] = nc.dram_tensor(nm, shp, dt, kind="ExternalOutput").ap()
        self.dram, self.taps = dram, taps

        es = ExitStack()
        with es:
            S = {}
            self.S = S

            def sb(name, shape, dt):
                S[name] = es.enter_context(nc.sbuf_tensor(name, shape, dt))

            sb("xres", [128, NT * D], F32)
            sb("kfm", [128, 2 * T], BF16)
            sb("vtm", [128, NT * KVH * HD], BF16)
            sb("wproj", [128, NCH * 1536], BF16)
            sb("amats", [128, NCH * 128], BF16)
            sb("bmats", [RANK, 2560], BF16)
            sb("usb", [RANK, 3 * 128], BF16)
            sb("scal", [128, 64], F32)
            sb("bn", [128, 32], F32)
            sb("s_cpA", [128, NT * HD], F32)
            sb("s_spA", [128, NT * HD], F32)
            sb("s_cpB", [128, NT * HD], F32)
            sb("s_spB", [128, NT * HD], F32)
            sb("gainb", [128, D], BF16)
            sb("gateb", [128, H], F32)
            sb("w1b", [128, D], BF16)
            sb("w2b", [128, D], BF16)
            sb("s_maskadd", [128, 128], F32)
            sb("s_identb", [128, 128], BF16)
            sb("s_identf", [128, 128], F32)
            sb("s_onesrow", [1, 128], F32)
            sb("gatef", [128, NCH * NE], F32)
            sb("wfull", [128, NT * NE], F32)
            sb("rowtmp", [1, D], F32)
            sb("ovl", [128, 16000], F32)
            if "tap_fqdbg" in self.debug_taps:
                sb("fqdbg", [128, 64], F32)
            self.ovl = S["ovl"]
            self.ovb = S["ovl"].bitcast(BF16)
            self.ovi = S["ovl"].bitcast(I32)

            self.psA = es.enter_context(nc.psum_tensor("psA", [128, 2048], F32))
            self.psB = es.enter_context(nc.psum_tensor("psB", [128, 2048], F32))
            self.psBb = self.psB.bitcast(BF16)

            block = es.enter_context(nc.Block())

            self.fakequant("Wq", D, "wsq")
            self.setup_consts()
            self.P.barrier()
            for wnm, nrows, scr in [("Wq", D, "wsq"), ("Wk", KVH * HD, "wsk"),
                                    ("Wv", KVH * HD, "wsv"), ("Wo", D, "wso")]:
                self.fakequant(wnm, nrows, scr)
                if f"tap_{scr}" in self.taps:
                    self.st(self.taps[f"tap_{scr}"], dram[scr], (scr,),
                            tag=f"tap{scr}")
                if "tap_fqdbg" in self.taps and wnm == "Wq":
                    self.st(self.taps["tap_fqdbg"], self.S["fqdbg"][:],
                            tuple(f"dbg:{tl}{x}" for tl in range(8) for x in "abc"),
                            tag="tapfqdbg")
            self.load_small_weights()
            self.P.barrier()
            for sq in range(NSEQ):
                self.sequence(sq)
            self.n_inst = self.P.emit(nc, es, block)
        return nc

    def setup_consts(self):
        S, dram = self.S, self.dram
        for nm in ("maskadd", "identb", "identf", "onesrow"):
            self.ld(S["s_" + nm][:], dram[nm], (nm,), tag="c_" + nm)
        for nm in ("cpA", "spA", "cpB", "spB"):
            self.ld(S["s_" + nm][:], dram[nm].rearrange("(t p) f -> p t f", p=128),
                    (nm,), tag="c_" + nm)
        self.ld(S["gatef"][:].rearrange("p (c e) -> p c e", e=NE),
                dram["gateT"].rearrange("(c p) e -> p c e", p=128),
                ("gatef",), tag="c_gate")

        def bcast(src_dram, width, rep_each, dst, tag):
            rt = S["rowtmp"]
            self.ld(rt[0:1, 0:width], src_dram, ("rowtmp",), rkeys=("rowtmp",),
                    tag="cb_" + tag)
            wr = width * rep_each
            if rep_each > 1:
                rta = rt[:]
                src = bass.AP(rta.tensor, rta.offset,
                              [[rta.ap[0][0], 1], [1, width], [0, rep_each]])

                def er(eng, s=src,
                       o=rt[0:1, 0:wr].rearrange("p (a b) -> p a b", a=width)):
                    return eng.tensor_copy(out=o, in_=s)
                self.v(er, r=("rowtmp",), w=("rowtmp",), tag="cbr_" + tag)
                sk = "rowtmp"
            else:
                sk = "rowtmp"
            for hh in range((wr + 511) // 512):
                w0, w1 = hh * 512, min(wr, hh * 512 + 512)

                def em(eng, w0=w0, w1=w1, rt=rt):
                    return eng.matmul(out=self.psB[:, 0:w1 - w0],
                                      lhsT=S["s_onesrow"][0:1, :], rhs=rt[0:1, w0:w1],
                                      start=True, stop=True)
                self.t(em, r=(sk, "onesrow"), w=("psB:4",), tag="cbm_" + tag)

                def ee(eng, w0=w0, w1=w1, d=S[dst]):
                    return eng.tensor_copy(out=d[:, w0:w1], in_=self.psB[:, 0:w1 - w0])
                self.v(ee, r=("psB:4",), w=(dst,), tag="cbe_" + tag)

        bcast(self.dram["q_gain"], D, 1, "gainb", "qg")
        bcast(self.dram["head_gate"], H, 1, "gateb", "hg")
        bcast(self.dram["ln1_w"], D, 1, "w1b", "w1")
        bcast(self.dram["ln2_w"], D, 1, "w2b", "w2")

    def load_small_weights(self):
        S, dram = self.S, self.dram
        am = S["amats"][:]
        for i, nm in enumerate(("qA", "kA", "vA", "pA")):
            dst = bass.AP(am.tensor, am.offset + i * 32,
                          [[am.ap[0][0], 128], [128, NCH], [1, 32]])
            self.ld(dst, dram[nm].rearrange("(c p) r -> p c r", p=128),
                    (f"am_{nm}",), q="gpsimd", tag="w_" + nm)
        for nm, o in (("qB", 0), ("kB", 1024), ("vB", 1280), ("pB", 1536)):
            wdt = dram[nm].shape[1]
            self.ld(S["bmats"][0:RANK, o:o + wdt], dram[nm], (f"bm_{nm}",),
                    q="gpsimd", tag="w_" + nm)
        self.AMK = ("am_qA", "am_kA", "am_vA", "am_pA")

    def fakequant(self, Wnm, nrows, scr):
        S, dram, ovl, ovb, ovi = self.S, self.dram, self.ovl, self.ovb, self.ovi
        FQ = self.FQ
        ntl = nrows // 128
        wt = ovl[:, FQ["wt"]:FQ["wt"] + 1024]
        y = ovl[:, FQ["y"]:FQ["y"] + 1024]
        q0f = ovl[:, FQ["q0f"]:FQ["q0f"] + 1024]
        t1 = ovl[:, FQ["t1"]:FQ["t1"] + 1024]
        qi = ovi[:, FQ["qi"]:FQ["qi"] + 1024]
        q1b = ovl[:, FQ["q1"]:FQ["q1"] + 1024]
        q2b = ovl[:, FQ["q2"]:FQ["q2"] + 1024]
        d1b = ovl[:, FQ["d1"]:FQ["d1"] + 1024]
        d2b = ovl[:, FQ["d2"]:FQ["d2"] + 1024]
        wdq = ovb[:, 2 * FQ["wdq"]:2 * FQ["wdq"] + 1024]
        asm = ovb[:, 2 * FQ["asm"]:2 * FQ["asm"] + NCH * nrows]
        for tl in range(ntl):
            self.ld(wt, dram[Wnm].rearrange("(n p) d -> n p d", p=128)[tl],
                    ("fq_wt",), rkeys=("fq_wt",), tag=f"fq_{Wnm}{tl}")
            bslot = self._bn
            self._bn ^= 1
            bn = S["bn"][:, bslot * 12:(bslot + 1) * 12]
            bnk = f"bn:{bslot}"

            def ebn(eng, b=bn):
                eng.bn_stats(out=b[:, 0:6], in_=wt[:, 0:512])
                return eng.bn_stats(out=b[:, 6:12], in_=wt[:, 512:1024])
            self.v(ebn, r=("fq_wt",), w=(bnk,), tag="fqbn")
            c = self.sslot(12)
            sc = S["scal"]
            K = lambda i: self.skey(c, i)
            var = self.sap(c, 1)
            std, am_ = self.sap(c, 2), self.sap(c, 3)
            ta, tb = self.sap(c, 4), self.sap(c, 5)
            s1, s2 = self.sap(c, 6), self.sap(c, 7)
            r1, r2 = self.sap(c, 8), self.sap(c, 9)
            e1, e2 = self.sap(c, 10), self.sap(c, 11)

            def eag(eng, b=bn, o=sc[:, c:c + 2]):
                return eng.bn_aggr(out=o, in_=b)
            self.v(eag, r=(bnk,), w=(K(0), K(1)), tag="fqag")

            def eln(eng, i=var, o=std):
                return eng.activation(out=o, in_=i, func=AF.Ln)
            self.a(eln, sr=(K(1),), w=(K(2),), tag="fqln")

            def eex(eng, o=std):
                return eng.activation(out=o, in_=o, func=AF.Exp, scale=0.5)
            self.a(eex, sr=(K(2),), w=(K(2),), tag="fqex")

            def eam(eng, o=am_):
                return eng.reduce_max(out=o, in_=wt, axis=AX.X,
                                      apply_absolute_value=True)
            self.v(eam, r=("fq_wt",), w=(K(3),), tag="fqam")

            def ea(eng, i=std, o=ta):
                return eng.tensor_scalar(out=o, in0=i, scalar1=1e-8, scalar2=12.85,
                                         op0=ALU.max, op1=ALU.mult)
            self.v(ea, sr=(K(2),), w=(K(4),), tag="fqa")

            def eb(eng, i=am_, o=tb):
                return eng.tensor_scalar(out=o, in0=i, scalar1=INV31, scalar2=INV31,
                                         op0=ALU.max, op1=ALU.mult)
            self.v(eb, sr=(K(3),), w=(K(5),), tag="fqb")

            def es2(eng, i=tb, o=s2):
                return eng.tensor_single_scalar(out=o, in_=i, scalar=INV31, op=ALU.max)
            self.v(es2, sr=(K(5),), w=(K(7),), tag="fqs2")

            def ec(eng, o=ta, i=tb):
                return eng.tensor_tensor(out=o, in0=o, in1=i, op=ALU.max)
            self.v(ec, sr=(K(4), K(5)), w=(K(4),), tag="fqc")

            def es1(eng, i=ta, o=s1):
                return eng.tensor_scalar(out=o, in0=i, scalar1=INV31, scalar2=INV31,
                                         op0=ALU.mult, op1=ALU.max)
            self.v(es1, sr=(K(4),), w=(K(6),), tag="fqs1")

            def er1(eng, i=s1, o=r1):
                return eng.reciprocal(out=o, in_=i)
            self.v(er1, sr=(K(6),), w=(K(8),), tag="fqr1")

            def er2(eng, i=s2, o=r2):
                return eng.reciprocal(out=o, in_=i)
            self.v(er2, sr=(K(7),), w=(K(9),), tag="fqr2")

            def quant(rap, rkey, qdst, qkey):
                def ey(eng, rr=rap):
                    return eng.tensor_single_scalar(out=y, in_=wt, scalar=rr, op=ALU.mult)
                self.v(ey, r=("fq_wt",), sr=(rkey,), w=("fq_y",), tag="fqy")

                def eci(eng):
                    return eng.tensor_copy(out=qi, in_=y)
                self.v(eci, r=("fq_y",), w=("fq_qi",), tag="fqci")

                def ecf(eng):
                    return eng.tensor_copy(out=q0f, in_=qi)
                self.v(ecf, r=("fq_qi",), w=("fq_q0",), tag="fqcf")

                def ed(eng):
                    return eng.tensor_tensor(out=y, in0=y, in1=q0f, op=ALU.subtract)
                self.v(ed, r=("fq_y", "fq_q0"), w=("fq_y",), tag="fqd")

                def egt(eng):
                    return eng.tensor_single_scalar(out=t1, in_=y, scalar=0.5,
                                                    op=ALU.is_gt)
                self.v(egt, r=("fq_y",), w=("fq_t1",), tag="fqgt")

                def eadd(eng):
                    return eng.tensor_tensor(out=q0f, in0=q0f, in1=t1, op=ALU.add)
                self.v(eadd, r=("fq_q0", "fq_t1"), w=("fq_q0",), tag="fqad")

                def elt(eng):
                    return eng.tensor_single_scalar(out=t1, in_=y, scalar=-0.5,
                                                    op=ALU.is_lt)
                self.v(elt, r=("fq_y",), w=("fq_t1",), tag="fqlt")

                def esub(eng):
                    return eng.tensor_tensor(out=q0f, in0=q0f, in1=t1, op=ALU.subtract)
                self.v(esub, r=("fq_q0", "fq_t1"), w=("fq_q0",), tag="fqsb")

                def ecl(eng, o=qdst):
                    return eng.tensor_scalar(out=o, in0=q0f, scalar1=QMAXF,
                                             scalar2=-QMAXF, op0=ALU.min, op1=ALU.max)
                self.v(ecl, r=("fq_q0",), w=(qkey,), tag="fqcl")

            quant(r1, K(8), q1b, "fq_q1")
            quant(r2, K(9), q2b, "fq_q2")

            def ed1(eng):
                return eng.scalar_tensor_tensor(out=d1b, in0=q1b, scalar=s1, in1=wt,
                                                op0=ALU.mult, op1=ALU.subtract)
            self.v(ed1, r=("fq_q1", "fq_wt"), sr=(K(6),), w=("fq_d1",), tag="fqd1")

            def ee1a(eng, b=bn):
                eng.bn_stats(out=b[:, 0:6], in_=d1b[:, 0:512])
                return eng.bn_stats(out=b[:, 6:12], in_=d1b[:, 512:1024])
            self.v(ee1a, r=("fq_d1",), w=(bnk,), tag="fqe1a")

            c1x = self.sslot(2)

            def ee1b(eng, b=bn, o=sc[:, c1x:c1x + 2]):
                return eng.bn_aggr(out=o, in_=b)
            self.v(ee1b, r=(bnk,), w=(self.skey(c1x), self.skey(c1x, 1)), tag="fqe1b")

            def ee1c(eng):
                return eng.scalar_tensor_tensor(out=e1, in0=self.sap(c1x),
                                                scalar=self.sap(c1x),
                                                in1=self.sap(c1x, 1),
                                                op0=ALU.mult, op1=ALU.add)
            self.v(ee1c, sr=(self.skey(c1x), self.skey(c1x, 1)), w=(K(10),),
                   tag="fqe1c")

            def ed2(eng):
                return eng.scalar_tensor_tensor(out=d2b, in0=q2b, scalar=s2, in1=wt,
                                                op0=ALU.mult, op1=ALU.subtract)
            self.v(ed2, r=("fq_q2", "fq_wt"), sr=(K(7),), w=("fq_d2",), tag="fqd2")

            bslot2 = self._bn
            self._bn ^= 1
            bn2 = S["bn"][:, bslot2 * 12:(bslot2 + 1) * 12]
            bnk2 = f"bn:{bslot2}"
            c2 = self.sslot(2)
            ea2, eb2 = self.sap(c2), self.sap(c2, 1)

            def ee2a(eng, b=bn2):
                eng.bn_stats(out=b[:, 0:6], in_=d2b[:, 0:512])
                return eng.bn_stats(out=b[:, 6:12], in_=d2b[:, 512:1024])
            self.v(ee2a, r=("fq_d2",), w=(bnk2,), tag="fqe2a")

            def ee2b(eng, b=bn2, o=sc[:, c2:c2 + 2]):
                return eng.bn_aggr(out=o, in_=b)
            self.v(ee2b, r=(bnk2,), w=(self.skey(c2), self.skey(c2, 1)), tag="fqe2b")

            def ee2c(eng):
                return eng.scalar_tensor_tensor(out=e2, in0=ea2, scalar=ea2, in1=eb2,
                                                op0=ALU.mult, op1=ALU.add)
            self.v(ee2c, sr=(self.skey(c2), self.skey(c2, 1)), w=(K(11),), tag="fqe2c")

            use1 = ta

            def eu(eng, o=use1):
                return eng.tensor_tensor(out=o, in0=e1, in1=e2, op=ALU.is_lt)
            self.v(eu, sr=(K(10), K(11)), w=(K(4),), tag="fqu")
            if "tap_fqdbg" in self.taps and Wnm == "Wq":
                def edbgu(eng, o=S["fqdbg"][:, tl * 8 + 6:tl * 8 + 8],
                          i=sc[:, c + 10:c + 12]):
                    return eng.tensor_copy(out=o, in_=i)
                self.v(edbgu, sr=(K(10), K(11)), w=(f"dbg:{tl}c",), tag="dbgu")

            if "tap_fqdbg" in self.taps and Wnm == "Wq":
                def edbgpre(eng, o=S["fqdbg"][:, tl * 8:tl * 8 + 4],
                            i=sc[:, c + 4:c + 8]):
                    return eng.tensor_copy(out=o, in_=i)
                self.v(edbgpre, sr=(K(4), K(5), K(6), K(7)), w=(f"dbg:{tl}a",),
                       tag="dbgpre")
            s16 = S["bn"].bitcast(F16)
            s12 = sc[:, c + 6:c + 8]

            def e1h(eng, o=s16[:, 48:50], i=s12):
                return eng.tensor_copy(out=o, in_=i)
            self.v(e1h, sr=(K(6), K(7)), w=("fq16:48",), tag="fqh")

            def e2h(eng, o=s12, i=s16[:, 48:50]):
                return eng.tensor_copy(out=o, in_=i)
            self.v(e2h, sr=("fq16:48",), w=(K(6), K(7)), tag="fqhb")
            if "tap_fqdbg" in self.taps and Wnm == "Wq":
                def edbgpost(eng, o=S["fqdbg"][:, tl * 8 + 4:tl * 8 + 6],
                             i=s12):
                    return eng.tensor_copy(out=o, in_=i)
                self.v(edbgpost, sr=(K(6), K(7)), w=(f"dbg:{tl}b",), tag="dbgpost")

            def edh1(eng):
                return eng.tensor_single_scalar(out=d1b, in_=q1b, scalar=s1, op=ALU.mult)
            self.v(edh1, r=("fq_q1",), sr=(K(6),), w=("fq_d1",), tag="fqD1")

            def edh2(eng):
                return eng.tensor_single_scalar(out=d2b, in_=q2b, scalar=s2, op=ALU.mult)
            self.v(edh2, r=("fq_q2",), sr=(K(7),), w=("fq_d2",), tag="fqD2")

            def edd(eng):
                return eng.tensor_tensor(out=d1b, in0=d1b, in1=d2b, op=ALU.subtract)
            self.v(edd, r=("fq_d1", "fq_d2"), w=("fq_d1",), tag="fqdd")

            def ew(eng):
                return eng.scalar_tensor_tensor(out=wdq, in0=d1b, scalar=use1, in1=d2b,
                                                op0=ALU.mult, op1=ALU.add)
            self.v(ew, r=("fq_d1", "fq_d2"), sr=(K(4),), w=("fq_wdq",), tag="fqw")

            for cg in range(2):
                def etr(eng, cg=cg):
                    inst = None
                    for c4 in range(4):
                        cc = cg * 4 + c4
                        inst = eng.transpose(
                            out=self.psBb[:, 1024 + c4 * 128:1024 + (c4 + 1) * 128],
                            in_=wdq[:, cc * 128:(cc + 1) * 128],
                            identity=S["s_identb"][:])
                    return inst
                self.t(etr, r=("fq_wdq", "identb"), w=("psB:5",), tag="fqtr")

                def ete(eng, cg=cg, tl=tl):
                    dst = bass.AP(asm.tensor, asm.offset + (cg * 4) * nrows + tl * 128,
                                  [[asm.ap[0][0], 128], [nrows, 4], [1, 128]])
                    return eng.tensor_copy(
                        out=dst,
                        in_=self.psBb[:, 1024:1536].rearrange("p (c f) -> p c f", c=4))
                self.v(ete, r=("psB:5",), w=(f"fqA:{tl}:{cg}",), tag="fqte")

        self.st(dram[scr].rearrange("(c p) f -> p c f", p=128),
                bass.AP(asm.tensor, asm.offset,
                        [[asm.ap[0][0], 128], [nrows, NCH], [1, nrows]]),
                tuple(f"fqA:{tl}:{cg}" for tl in range(ntl) for cg in range(2)),
                wkeys=(scr,), tag=f"fqsp{Wnm}")
        self.P.barrier()

    def sequence(self, sq):
        S, dram = self.S, self.dram
        self.phase_attn(sq, "A")
        if "tap_vA" in self.taps and sq == 0:
            self.st(self.taps["tap_vA"], S["vtm"][:],
                    tuple(f"vtm:{tl}" for tl in range(NT)), tag="tapvA")
        if "tap_kA" in self.taps and sq == 0:
            self.st(self.taps["tap_kA"], S["kfm"][:],
                    tuple(f"kfm:{tl}" for tl in range(NT)), tag="tapkA")
        for sl in range(NSLAB):
            self.st(dram["x1scr"][sl * SLAB:(sl + 1) * SLAB, :]
                    .rearrange("(a p) d -> p a d", p=128),
                    S["xres"][:, sl * TPS * D:(sl + 1) * TPS * D]
                    .rearrange("p (a d) -> p a d", d=D),
                    tuple(f"xres:{sl * TPS + i}" for i in range(TPS)),
                    wkeys=(f"x1scr:{sl}",), tag=f"spill{sq}_{sl}")
        if "tap_x1" in self.taps and sq == 0:
            for sl in range(NSLAB):
                self.st(self.taps["tap_x1"][sl * SLAB:(sl + 1) * SLAB, :]
                        .rearrange("(a p) d -> p a d", p=128),
                        S["xres"][:, sl * TPS * D:(sl + 1) * TPS * D]
                        .rearrange("p (a d) -> p a d", d=D),
                        tuple(f"xres:{sl * TPS + i}" for i in range(TPS)),
                        tag="tapx1")
        for tl in range(NT):
            xt = S["xres"][:, tl * D:(tl + 1) * D]
            mk, mean, rk, rstd = self.stats(xt, (f"xres:{tl}",), "ln", tag=f"lnB{tl}")

            def eap(eng, x=xt, m=mean, r=rstd):
                return eng.tensor_scalar(out=x, in0=x, scalar1=m, scalar2=r,
                                         op0=ALU.subtract, op1=ALU.mult)
            self.v(eap, r=(f"xres:{tl}", f"x1scr:{tl // TPS}"), sr=(mk, rk),
                   w=(f"xres:{tl}",), tag=f"lnBa{tl}")
        if "tap_fin" in self.taps and sq == 0:
            self.tap_xres("tap_fin")
        self.P.barrier()
        self.phase_attn(sq, "B")
        if "tap_u2" in self.taps and sq == 0:
            self.tap_xres("tap_u2")
        self.P.barrier()
        self.router_and_h2(sq)
        if "tap_wfull" in self.taps and sq == 0:
            self.st(self.taps["tap_wfull"], self.S["wfull"][:],
                    tuple(f"wfull:{tl}" for tl in range(NT)), tag="tapwf")
        if "tap_h2" in self.taps and sq == 0:
            hfm = self.mvb("hfm", NCH * T)
            self.st(self.taps["tap_h2"],
                    bass.AP(hfm.tensor, hfm.offset,
                            [[hfm.ap[0][0], 128], [T, NCH], [1, T]]),
                    tuple(f"hfm:{tl}:{cg}" for tl in range(NT) for cg in range(2)),
                    tag="taph2")
        self.P.barrier()
        self.moe(sq)
        if "tap_fout" in self.taps and sq == 0:
            self.tap_xres("tap_fout")
        self.P.barrier()
        self.final_out(sq)
        self.P.barrier()

    def tap_xres(self, nm):
        S = self.S
        for sl in range(NSLAB):
            self.st(self.taps[nm][sl * SLAB:(sl + 1) * SLAB, :]
                    .rearrange("(a p) d -> p a d", p=128),
                    S["xres"][:, sl * TPS * D:(sl + 1) * TPS * D]
                    .rearrange("p (a d) -> p a d", d=D),
                    tuple(f"xres:{sl * TPS + i}" for i in range(TPS)),
                    tag=f"tx{nm}{sl}")

    def phase_attn(self, sq, phase):
        S, dram = self.S, self.dram
        self._sq = sq
        A = phase == "A"
        wp = S["wproj"][:]
        wT = self.cvb("WT", NCH * 1024)
        if A:
            for nm, off, wid in (("wsq", 0, 1024), ("wsk", 1024, 256),
                                 ("wsv", 1280, 256)):
                dst = bass.AP(wp.tensor, wp.offset + off,
                              [[wp.ap[0][0], 128], [1536, NCH], [1, wid]])
                self.ld(dst, dram[nm].rearrange("(c p) f -> p c f", p=128),
                        ("wproj",), rkeys=(nm, "wproj"), tag=f"lw{nm}{sq}")
            dstT = bass.AP(wT.tensor, wT.offset,
                           [[wT.ap[0][0], 128], [1024, NCH], [1, 1024]])
            self.ld(dstT, dram["wso"].rearrange("(c p) f -> p c f", p=128),
                    ("WT",), rkeys=("wso", "WT"), tag=f"lwso{sq}")
            cpt, spt = "cpA", "spA"
        else:
            dst = bass.AP(wp.tensor, wp.offset,
                          [[wp.ap[0][0], 128], [1536, NCH], [1, 1536]])
            self.ld(dst, dram["wqkvT"].rearrange("(c p) f -> p c f", p=128),
                    ("wproj",), rkeys=("wproj",), q="gpsimd", tag=f"lwqkv{sq}")
            dstT = bass.AP(wT.tensor, wT.offset,
                           [[wT.ap[0][0], 128], [1024, NCH], [1, 1024]])
            self.ld(dstT, dram["wo2T"].rearrange("(c p) f -> p c f", p=128),
                    ("WT",), rkeys=("WT",), q="gpsimd", tag=f"lwo2{sq}")
            cpt, spt = "cpB", "spB"

        if A:
            for sl in range(NSLAB):
                keys = tuple(f"xres:{sl * TPS + i}" for i in range(TPS))
                self.ld(S["xres"][:, sl * TPS * D:(sl + 1) * TPS * D]
                        .rearrange("p (a d) -> p a d", d=D),
                        dram["x"][sq * T + sl * SLAB: sq * T + (sl + 1) * SLAB, :]
                        .rearrange("(a p) d -> p a d", p=128),
                        keys, rkeys=keys, tag=f"lx{sq}_{sl}")

        qfm = self.cvb("qfm", 2048)
        yfm = self.cvb("yfm", 2048)
        Pb = self.cvb("P", 2048)
        Ptb = self.cvb("Pt", 4096)
        for sl in range(NSLAB):
            for ti in range(TPS):
                self.qkv_tile(phase, sl, ti, qfm, cpt, spt)
            if ("tap_qf" in self.taps and A and sl == 4 and self._sq == 0):
                self.st(self.taps["tap_qf"], qfm,
                        tuple(f"qfm:{t2}:{cg}" for t2 in range(TPS)
                              for cg in range(2)), tag="tapqf")
            nkz = (sl + 1) * TPS

            def ez(eng, o=Ptb[:, (nkz - 1) * 256:(nkz - 1) * 256 + 128]):
                return eng.tensor_single_scalar(out=o, in_=S["s_identb"][:],
                                                scalar=0.0, op=ALU.mult)
            self.v(ez, r=("identb",), w=(f"Pt:{nkz - 1}",), tag=f"{phase}Pz{sl}")
            for h in range(H):
                self.attn_head(phase, sl, h, qfm, yfm, Pb, Ptb)
            if "tap_yf" in self.taps and sq == 0 and A:
                self.st(self.taps["tap_yf"][sl * 128:(sl + 1) * 128, :],
                        self.cvb("yfm", 2048),
                        tuple(f"yfm:{h}" for h in range(H)), tag=f"tapyf{sl}")
            for ti in range(TPS):
                self.oproj_tile(phase, sl, ti, yfm, wT)

    def qkv_tile(self, phase, sl, ti, qfm, cpt, spt):
        S = self.S
        A = phase == "A"
        psA, psB, psBb = self.psA, self.psB, self.psBb
        tl = sl * TPS + ti
        slot = tl % 2
        xt = S["xres"][:, tl * D:(tl + 1) * D]
        ltm = self.cvb("ltm", 2048)[:, slot * 1024:slot * 1024 + 1024]
        lfm = self.cvb("lfm", 2048)[:, slot * 1024:slot * 1024 + 1024]
        qtm = self.cvb("qtm", 2048)[:, slot * 1024:slot * 1024 + 1024]
        ktm = self.cvb("ktm", 512)[:, slot * 256:slot * 256 + 256]
        rsA = self.cvf("rsA", 1024)
        rsB = self.cvf("rsB", 1024)
        kl = lambda nm: f"{nm}:{slot}"
        mk, mean, rk, rstd = self.stats(xt, (f"xres:{tl}",), "ln" if A else "rms",
                                        tag=f"{phase}s{tl}")
        if A:
            def eap(eng, x=xt, o=ltm, m=mean, r=rstd):
                return eng.tensor_scalar(out=o, in0=x, scalar1=m, scalar2=r,
                                         op0=ALU.subtract, op1=ALU.mult)
            self.v(eap, r=(f"xres:{tl}",), sr=(mk, rk), w=(kl("ltm"),), tag=f"Aln{tl}")
        else:
            def eap(eng, x=xt, o=ltm, r=rstd):
                return eng.scalar_tensor_tensor(out=o, in0=x, scalar=r, in1=S["w1b"][:],
                                                op0=ALU.mult, op1=ALU.mult)
            self.v(eap, r=(f"xres:{tl}", "w1b"), sr=(rk,), w=(kl("ltm"),), tag=f"Brm{tl}")
        for cg in range(2):
            def etr(eng, cg=cg, i=ltm):
                inst = None
                for c4 in range(4):
                    c = cg * 4 + c4
                    inst = eng.transpose(
                        out=psBb[:, 1024 + c4 * 128:1024 + (c4 + 1) * 128],
                        in_=i[:, c * 128:(c + 1) * 128], identity=S["s_identb"][:])
                return inst
            self.t(etr, r=(kl("ltm"), "identb"), w=("psB:5",), tag=f"{phase}ltr{tl}")

            def ete(eng, cg=cg, o=lfm):
                return eng.tensor_copy(out=o[:, cg * 512:(cg + 1) * 512],
                                       in_=psBb[:, 1024:1536])
            self.v(ete, r=("psB:5",), w=(kl("lfm"),), tag=f"{phase}lte{tl}")
        lf = lambda c: lfm[:, c * 128:(c + 1) * 128]
        if A:
            pairs = [(S["amats"][:, c * 128:c * 128 + 96], lf(c)) for c in range(NCH)]
            self.mmg(psB[0:96, 0:128], pairs, self.AMK[:3] + (kl("lfm"),), ("psB:4",),
                     tag=f"Au{tl}")
            for i in range(3):
                def eue(eng, i=i):
                    return eng.tensor_copy(out=S["usb"][0:RANK, i * 128:(i + 1) * 128],
                                           in_=psB[i * 32:(i + 1) * 32, 0:128])
                self.v(eue, r=("psB:4",), w=(f"usb:{i}",), tag=f"Aue{tl}_{i}")
        wp = S["wproj"][:]
        nout = 1024 if A else 1536
        for g in range(nout // 512):
            pairs = [(lf(c), wp[:, c * 1536 + g * 512: c * 1536 + (g + 1) * 512])
                     for c in range(NCH)]
            rk2 = [kl("lfm"), "wproj"]
            if A:
                pairs.append((S["usb"][0:RANK, 0:128],
                              S["bmats"][0:RANK, g * 512:(g + 1) * 512]))
                rk2 += ["usb:0", "bm_qB"]
            self.mmg(psA[:, g * 512:(g + 1) * 512], pairs, rk2, (f"psA:{g}",),
                     tag=f"{phase}q{tl}g{g}")
        if A:
            pairs = [(lf(c), wp[:, c * 1536 + 1024: c * 1536 + 1280])
                     for c in range(NCH)]
            pairs.append((S["usb"][0:RANK, 128:256], S["bmats"][0:RANK, 1024:1280]))
            self.mmg(psA[:, 1024:1280], pairs, (kl("lfm"), "wproj", "usb:1", "bm_kB"),
                     ("psA:2",), tag=f"Ak{tl}")
            pairs = [(lf(c), wp[:, c * 1536 + 1280: c * 1536 + 1536])
                     for c in range(NCH)]
            pairs.append((S["usb"][0:RANK, 256:384], S["bmats"][0:RANK, 1280:1536]))
            self.mmg(psA[:, 1280:1536], pairs, (kl("lfm"), "wproj", "usb:2", "bm_vB"),
                     ("psA:2",), tag=f"Av{tl}")
        cp4 = self.tab(cpt, tl, 4)
        sp4 = self.tab4(spt, tl, 4)
        for g in range(4):
            def era(eng, g=g):
                return eng.tensor_tensor(
                    out=rsA[:, g * 256:(g + 1) * 256]
                    .rearrange("p (a b) -> p a b", a=4),
                    in0=psA[:, g * 256:(g + 1) * 256]
                    .rearrange("p (a b) -> p a b", a=4),
                    in1=cp4, op=ALU.mult)
            self.v(era, r=("psA:0", "psA:1", cpt), w=("rsA",),
                   tag=f"{phase}ra{tl}g{g}")

            def erb(eng, g=g):
                return eng.tensor_tensor(
                    out=self.shufw(rsB[:, g * 256:(g + 1) * 256], 4),
                    in0=self.g64(psA[:, g * 256:(g + 1) * 256], 4),
                    in1=sp4, op=ALU.mult)
            self.v(erb, r=("psA:0", "psA:1", spt), w=("rsB",),
                   tag=f"{phase}rb{tl}g{g}")
        for a in range(2):
            def eq(eng, a=a, o=qtm):
                op = bass.AP(o.tensor, o.offset + a * 64,
                             [[o.ap[0][0], 128], [128, 8], [1, 64]])
                ia = bass.AP(rsA.tensor, rsA.offset + a * 512,
                             [[rsA.ap[0][0], 128], [64, 8], [1, 64]])
                ib = bass.AP(rsB.tensor, rsB.offset + a * 512,
                             [[rsB.ap[0][0], 128], [64, 8], [1, 64]])
                return eng.tensor_tensor(out=op, in0=ia, in1=ib, op=ALU.add)
            self.v(eq, r=("rsA", "rsB"), w=(kl("qtm"),), tag=f"{phase}rq{tl}a{a}")
        if A:
            def eg(eng, o=qtm):
                return eng.tensor_tensor(out=o, in0=o, in1=S["gainb"][:], op=ALU.mult)
            self.v(eg, r=(kl("qtm"), "gainb"), w=(kl("qtm"),), tag=f"Agn{tl}")
        cpk = self.tab(cpt, tl, KVH)
        spk4 = self.tab4(spt, tl, KVH)

        def eka(eng):
            return eng.tensor_tensor(
                out=rsA[:, 0:256].rearrange("p (a b) -> p a b", a=KVH),
                in0=psA[:, 1024:1280].rearrange("p (a b) -> p a b", a=KVH),
                in1=cpk, op=ALU.mult)
        self.v(eka, r=("psA:2", cpt), w=("rsA",), tag=f"{phase}ka{tl}")

        def ekb(eng):
            return eng.tensor_tensor(out=self.shufw(rsB[:, 0:256], KVH),
                                     in0=self.g64(psA[:, 1024:1280], KVH),
                                     in1=spk4, op=ALU.mult)
        self.v(ekb, r=("psA:2", spt), w=("rsB",), tag=f"{phase}kb{tl}")

        def ek(eng, o=ktm):
            op = bass.AP(o.tensor, o.offset,
                         [[o.ap[0][0], 128], [64, 2], [128, 2], [1, 64]])
            ia = bass.AP(rsA.tensor, rsA.offset,
                         [[rsA.ap[0][0], 128], [128, 2], [64, 2], [1, 64]])
            ib = bass.AP(rsB.tensor, rsB.offset,
                         [[rsB.ap[0][0], 128], [128, 2], [64, 2], [1, 64]])
            return eng.tensor_tensor(out=op, in0=ia, in1=ib, op=ALU.add)
        self.v(ek, r=("rsA", "rsB"), w=(kl("ktm"),), tag=f"{phase}rk{tl}")

        def ev(eng, o=S["vtm"][:, tl * 256:(tl + 1) * 256]):
            return eng.activation(out=o, in_=psA[:, 1280:1536], func=AF.Copy)
        self.a(ev, r=("psA:2",), w=(f"vtm:{tl}",), tag=f"{phase}ve{tl}")
        for cg in range(2):
            def etq(eng, cg=cg, i=qtm):
                inst = None
                for c4 in range(4):
                    c = cg * 4 + c4
                    inst = eng.transpose(
                        out=psBb[:, 1024 + c4 * 128:1024 + (c4 + 1) * 128],
                        in_=i[:, c * 128:(c + 1) * 128], identity=S["s_identb"][:])
                return inst
            self.t(etq, r=(kl("qtm"), "identb"), w=("psB:5",), tag=f"{phase}qtr{tl}")

            def ete2(eng, cg=cg, ti=ti):
                dst = bass.AP(qfm.tensor, qfm.offset + (cg * 4) * 256 + ti * 128,
                              [[qfm.ap[0][0], 128], [256, 4], [1, 128]])
                return eng.tensor_copy(
                    out=dst, in_=psBb[:, 1024:1536].rearrange("p (c f) -> p c f", c=4))
            self.v(ete2, r=("psB:5",), w=(f"qfm:{ti}:{cg}",), tag=f"{phase}qte{tl}")
        def etk(eng, i=ktm):
            inst = None
            for c in range(2):
                inst = eng.transpose(out=psBb[:, 1024 + c * 128:1024 + (c + 1) * 128],
                                     in_=i[:, c * 128:(c + 1) * 128],
                                     identity=S["s_identb"][:])
            return inst
        self.t(etk, r=(kl("ktm"), "identb"), w=("psB:5",), tag=f"{phase}ktr{tl}")

        def etke(eng, tl=tl):
            kf = S["kfm"][:]
            dst = bass.AP(kf.tensor, kf.offset + tl * 128,
                          [[kf.ap[0][0], 128], [T, 2], [1, 128]])
            return eng.tensor_copy(
                out=dst, in_=psBb[:, 1024:1280].rearrange("p (c f) -> p c f", c=2))
        self.v(etke, r=("psB:5",), w=(f"kfm:{tl}",), tag=f"{phase}kte{tl}")

    def attn_head(self, phase, sl, h, qfm, yfm, Pb, Ptb):
        S = self.S
        A = phase == "A"
        psA, psB, psBb = self.psA, self.psB, self.psBb
        kv = h // (H // KVH)
        nk = (sl + 1) * TPS
        kfmk = tuple(f"kfm:{i}" for i in range(nk))
        for ti in range(TPS):
            tl = sl * TPS + ti
            w = (tl + 1) * 128
            nb = (w + 511) // 512
            skeys = tuple(f"psA:{i}" for i in range(nb))
            qh = qfm[(h // 8) * 64:(h // 8) * 64 + 64,
                     (h % 8) * 256 + ti * 128:(h % 8) * 256 + ti * 128 + 128]
            for b0 in range(0, w, 512):
                b1 = min(w, b0 + 512)

                def esc(eng, b0=b0, b1=b1, qh=qh):
                    kf = S["kfm"]
                    kh = kf[(kv // 2) * 64:(kv // 2) * 64 + 64,
                            (kv % 2) * T + b0:(kv % 2) * T + b1]
                    return eng.matmul(out=psA[:, b0:b1], lhsT=qh, rhs=kh,
                                      start=True, stop=True)
                self.t(esc, r=(f"qfm:{ti}:0", f"qfm:{ti}:1") + kfmk,
                       w=(f"psA:{b0 // 512}",), tag=f"{phase}S{h}_{tl}")

            def emk(eng, w=w):
                return eng.tensor_tensor(out=psA[:, w - 128:w], in0=psA[:, w - 128:w],
                                         in1=S["s_maskadd"][:], op=ALU.add)
            self.v(emk, r=skeys + ("maskadd",), w=(f"psA:{(w - 1) // 512}",),
                   tag=f"{phase}mk{h}_{tl}")
            c = self.sslot(4)
            m, bias = self.sap(c), self.sap(c, 1)
            ssum, rg = self.sap(c, 2), self.sap(c, 3)

            def emx(eng, w=w, o=m):
                return eng.reduce_max(out=o, in_=psA[:, 0:w], axis=AX.X)
            self.v(emx, r=skeys, w=(self.skey(c),), tag=f"{phase}mx{h}_{tl}")

            def ebi(eng, o=bias):
                return eng.tensor_single_scalar(out=o, in_=m, scalar=-SOFT_SCALE,
                                                op=ALU.mult)
            self.v(ebi, sr=(self.skey(c),), w=(self.skey(c, 1),), tag=f"{phase}bi{h}_{tl}")

            def eex(eng, w=w, b=bias):
                return eng.activation(out=Pb[:, 0:w], in_=psA[:, 0:w], func=AF.Exp,
                                      scale=SOFT_SCALE, bias=b)
            self.a(eex, r=skeys, sr=(self.skey(c, 1),), w=("Pb",),
                   tag=f"{phase}ex{h}_{tl}")

            def esm(eng, w=w, s=ssum):
                return eng.reduce_sum(out=s, in_=Pb[:, 0:w], axis=AX.X)
            self.v(esm, r=("Pb",), w=(self.skey(c, 2),), tag=f"{phase}sm{h}_{tl}")

            def erc(eng, o=rg, s=ssum):
                return eng.reciprocal(out=o, in_=s)
            self.v(erc, sr=(self.skey(c, 2),), w=(self.skey(c, 3),),
                   tag=f"{phase}rc{h}_{tl}")
            if A:
                def egg(eng, o=rg, h=h):
                    return eng.tensor_tensor(out=o, in0=o, in1=S["gateb"][:, h:h + 1],
                                             op=ALU.mult)
                self.v(egg, sr=(self.skey(c, 3),), r=("gateb",), w=(self.skey(c, 3),),
                       tag=f"Agg{h}_{tl}")

            def enr(eng, w=w, rr=rg):
                return eng.tensor_single_scalar(out=Pb[:, 0:w], in_=Pb[:, 0:w],
                                                scalar=rr, op=ALU.mult)
            self.v(enr, r=("Pb",), sr=(self.skey(c, 3),), w=("Pb",),
                   tag=f"{phase}nr{h}_{tl}")
            if ("tap_Pb" in self.taps and A and sl == 4 and h == 6 and ti == 0
                    and self._sq == 0):
                self.st(self.taps["tap_Pb"], Pb[:, 0:1280], ("Pb",), tag="tapPb")
            nch = w // 128
            for cg in range((nch + 3) // 4):
                c0, c1 = cg * 4, min(nch, cg * 4 + 4)

                def etp(eng, c0=c0, c1=c1):
                    inst = None
                    for cc in range(c0, c1):
                        inst = eng.transpose(
                            out=psBb[:, 1024 + (cc - c0) * 128:1024 + (cc - c0 + 1) * 128],
                            in_=Pb[:, cc * 128:(cc + 1) * 128], identity=S["s_identb"][:])
                    return inst
                self.t(etp, r=("Pb", "identb"), w=("psB:5",), tag=f"{phase}Pt{h}_{tl}")

                def etpe(eng, c0=c0, c1=c1, ti=ti):
                    dst = bass.AP(Ptb.tensor, Ptb.offset + c0 * 256 + ti * 128,
                                  [[Ptb.ap[0][0], 128], [256, c1 - c0], [1, 128]])
                    return eng.tensor_copy(
                        out=dst, in_=psBb[:, 1024:1024 + (c1 - c0) * 128]
                        .rearrange("p (c f) -> p c f", c=c1 - c0))
                self.v(etpe, r=("psB:5",),
                       w=tuple(f"Pt:{cc}" for cc in range(c0, c1)),
                       tag=f"{phase}Pe{h}_{tl}")
        if ("tap_Pt" in self.taps and A and sl == 4 and h == 6 and self._sq == 0):
            self.st(self.taps["tap_Pt"], Ptb[:, 0:nk * 256],
                    tuple(f"Pt:{cc}" for cc in range(nk)), tag="tapPt")
        pairs = []
        rks = []
        for cc in range(nk):
            vt = S["vtm"][:, cc * 256 + kv * 64: cc * 256 + kv * 64 + 64]
            pairs.append((vt, Ptb[:, cc * 256:(cc + 1) * 256]))
            rks += [f"vtm:{cc}", f"Pt:{cc}"]
        self.mmg(psB[0:64, 1536:1792], pairs, rks, ("psB:7",), tag=f"{phase}PV{h}_{sl}")

        def eye(eng, h=h):
            dst = yfm[(h % 2) * 64:(h % 2) * 64 + 64,
                      (h // 2) * 256:(h // 2) * 256 + 256]
            return eng.tensor_copy(out=dst, in_=psB[0:64, 1536:1792])
        self.v(eye, r=("psB:7",), w=(f"yfm:{h}",), tag=f"{phase}ye{h}_{sl}")

    def oproj_tile(self, phase, sl, ti, yfm, wT):
        S = self.S
        A = phase == "A"
        psA, psB = self.psA, self.psB
        tl = sl * TPS + ti
        ykeys = tuple(f"yfm:{h}" for h in range(H))
        yf = lambda c: yfm[:, c * 256 + ti * 128: c * 256 + ti * 128 + 128]
        if A:
            pairs = [(S["amats"][:, c * 128 + 96:c * 128 + 128], yf(c))
                     for c in range(NCH)]
            self.mmg(psB[0:32, 0:128], pairs, ("am_pA",) + ykeys, ("psB:4",),
                     tag=f"Aup{tl}")

            def eue(eng):
                return eng.tensor_copy(out=S["usb"][0:RANK, 0:128], in_=psB[0:32, 0:128])
            self.v(eue, r=("psB:4",), w=("usb:0",), tag=f"Aupe{tl}")
        for g in range(2):
            pairs = [(yf(c), wT[:, c * 1024 + g * 512: c * 1024 + (g + 1) * 512])
                     for c in range(NCH)]
            rk2 = list(ykeys) + ["WT"]
            if A:
                pairs.append((S["usb"][0:RANK, 0:128],
                              S["bmats"][0:RANK, 1536 + g * 512:1536 + (g + 1) * 512]))
                rk2 += ["usb:0", "bm_pB"]
            self.mmg(psA[:, g * 512:(g + 1) * 512], pairs, rk2, (f"psA:{g}",),
                     tag=f"{phase}o{tl}g{g}")
        xt = S["xres"][:, tl * D:(tl + 1) * D]
        if A:
            mk, mean, rk, rstd = self.stats(psA[:, 0:1024], ("psA:0", "psA:1"), "ln",
                                            tag=f"Ao{tl}")
            rsA = self.cvf("rsA", 1024)

            def efin(eng, m=mean, r=rstd):
                return eng.tensor_scalar(out=rsA, in0=psA[:, 0:1024], scalar1=m,
                                         scalar2=r, op0=ALU.subtract, op1=ALU.mult)
            self.v(efin, r=("psA:0", "psA:1"), sr=(mk, rk), w=("rsA",), tag=f"Aof{tl}")

            def eadd(eng, x=xt):
                return eng.tensor_tensor(out=x, in0=x, in1=rsA, op=ALU.add)
            self.v(eadd, r=(f"xres:{tl}", "rsA"), w=(f"xres:{tl}",), tag=f"Aoa{tl}")
        else:
            def eadd(eng, x=xt):
                return eng.tensor_tensor(out=x, in0=x, in1=psA[:, 0:1024], op=ALU.add)
            self.v(eadd, r=(f"xres:{tl}", "psA:0", "psA:1"), w=(f"xres:{tl}",),
                   tag=f"Boa{tl}")

    def router_and_h2(self, sq):
        S = self.S
        psA, psB = self.psA, self.psB
        hfm = self.mvb("hfm", NCH * T)
        h2f = self.mvf("h2f", 1024)
        htm = self.mvb("htm", 1024)
        hfmf = self.mvf("hfmf", 1024)
        for tl in range(NT):
            xt = S["xres"][:, tl * D:(tl + 1) * D]
            mk, mean, rk, rstd = self.stats(xt, (f"xres:{tl}",), "rms", tag=f"h2{tl}")

            def ehf(eng, x=xt, r=rstd):
                return eng.scalar_tensor_tensor(out=h2f, in0=x, scalar=r,
                                                in1=S["w2b"][:], op0=ALU.mult,
                                                op1=ALU.mult)
            self.v(ehf, r=(f"xres:{tl}", "w2b"), sr=(rk,), w=("h2f",), tag=f"h2f{tl}")

            def ehb(eng):
                return eng.tensor_copy(out=htm, in_=h2f)
            self.v(ehb, r=("h2f",), w=("htm",), tag=f"h2b{tl}")
            for cg in range(2):
                def etr(eng, cg=cg):
                    inst = None
                    for c4 in range(4):
                        c = cg * 4 + c4
                        inst = eng.transpose(
                            out=self.psBb[:, 1024 + c4 * 128:1024 + (c4 + 1) * 128],
                            in_=htm[:, c * 128:(c + 1) * 128], identity=S["s_identb"][:])
                    return inst
                self.t(etr, r=("htm", "identb"), w=("psB:5",), tag=f"h2tr{tl}")

                def ete(eng, cg=cg, tl=tl):
                    dst = bass.AP(hfm.tensor, hfm.offset + (cg * 4) * T + tl * 128,
                                  [[hfm.ap[0][0], 128], [T, 4], [1, 128]])
                    return eng.tensor_copy(
                        out=dst,
                        in_=self.psBb[:, 1024:1536].rearrange("p (c f) -> p c f", c=4))
                self.v(ete, r=("psB:5",), w=(f"hfm:{tl}:{cg}",), tag=f"h2te{tl}")
            for cg in range(2):
                def etf(eng, cg=cg):
                    inst = None
                    for c4 in range(4):
                        c = cg * 4 + c4
                        inst = eng.transpose(out=psB[:, c4 * 128:(c4 + 1) * 128],
                                             in_=h2f[:, c * 128:(c + 1) * 128],
                                             identity=S["s_identf"][:])
                    return inst
                self.t(etf, r=("h2f", "identf"), w=("psB:4",), tag=f"rtt{tl}")

                def etfe2(eng, cg=cg):
                    return eng.tensor_copy(out=hfmf[:, cg * 512:(cg + 1) * 512],
                                           in_=psB[:, 0:512])
                self.v(etfe2, r=("psB:4",), w=("hfmf",), tag=f"rte{tl}")
            pairs = [(hfmf[:, c * 128:(c + 1) * 128],
                      S["gatef"][:, c * NE:(c + 1) * NE]) for c in range(NCH)]
            self.mmg(psB[:, 1536:1536 + NE], pairs, ("hfmf", "gatef"), ("psB:7",),
                     tag=f"rl{tl}")
            c = self.sslot(8)
            lg = S["scal"][:, c:c + 8]
            lgk = tuple(self.skey(c, i) for i in range(8))

            def ele(eng, lg=lg):
                return eng.tensor_copy(out=lg, in_=psB[:, 1536:1536 + NE])
            self.v(ele, r=("psB:7",), w=lgk, tag=f"rle{tl}")
            self.router_tile(tl, lg, lgk)

    def router_tile(self, tl, lg, lgk):
        S = self.S
        c = self.sslot(6)
        m1, m2 = self.sap(c), self.sap(c, 1)
        den, rden = self.sap(c, 2), self.sap(c, 3)
        m1n = self.sap(c, 4)
        p8 = S["wfull"][:, tl * NE:(tl + 1) * NE]
        tmp8 = S["bn"][:, 24:32]

        def em1(eng):
            return eng.reduce_max(out=m1, in_=lg, axis=AX.X)
        self.v(em1, sr=lgk, w=(self.skey(c),), tag=f"rm1{tl}")

        def emn(eng):
            return eng.tensor_single_scalar(out=m1n, in_=m1, scalar=-1.0, op=ALU.mult)
        self.v(emn, sr=(self.skey(c),), w=(self.skey(c, 4),), tag=f"rmn{tl}")

        def epx(eng):
            return eng.activation(out=p8, in_=lg, func=AF.Exp, scale=1.0, bias=m1n)
        self.a(epx, r=lgk, sr=(self.skey(c, 4),), w=(f"wfull:{tl}",), tag=f"rpx{tl}")

        def eiq(eng):
            return eng.tensor_single_scalar(out=tmp8, in_=lg, scalar=m1, op=ALU.is_equal)
        self.v(eiq, r=lgk, sr=(self.skey(c),), w=("rt8",), tag=f"riq{tl}")

        def emsk(eng):
            return eng.scalar_tensor_tensor(out=tmp8, in0=tmp8, scalar=-65504.0, in1=lg,
                                            op0=ALU.mult, op1=ALU.add)
        self.v(emsk, r=("rt8",) + lgk, w=("rt8",), tag=f"rms{tl}")

        def em2(eng):
            return eng.reduce_max(out=m2, in_=tmp8, axis=AX.X)
        self.v(em2, r=("rt8",), w=(self.skey(c, 1),), tag=f"rm2{tl}")

        def ekp(eng):
            return eng.tensor_single_scalar(out=tmp8, in_=lg, scalar=m2, op=ALU.is_ge)
        self.v(ekp, r=lgk, sr=(self.skey(c, 1),), w=("rt8",), tag=f"rkp{tl}")

        def ewr(eng):
            return eng.tensor_tensor(out=p8, in0=p8, in1=tmp8, op=ALU.mult)
        self.v(ewr, r=(f"wfull:{tl}", "rt8"), w=(f"wfull:{tl}",), tag=f"rwr{tl}")

        def ews(eng):
            return eng.reduce_sum(out=den, in_=p8, axis=AX.X)
        self.v(ews, r=(f"wfull:{tl}",), w=(self.skey(c, 2),), tag=f"rws{tl}")

        def erd(eng):
            return eng.reciprocal(out=rden, in_=den)
        self.v(erd, sr=(self.skey(c, 2),), w=(self.skey(c, 3),), tag=f"rrd{tl}")

        def ewf(eng):
            return eng.tensor_single_scalar(out=p8, in_=p8, scalar=rden, op=ALU.mult)
        self.v(ewf, r=(f"wfull:{tl}",), sr=(self.skey(c, 3),), w=(f"wfull:{tl}",),
               tag=f"rwf{tl}")

    def moe(self, sq):
        S, dram = self.S, self.dram
        psA, psB = self.psA, self.psB
        hfm = self.mvb("hfm", NCH * T)
        gbuf = self.mvb("gbuf", 8192)
        hkeys = tuple(f"hfm:{tl}:{cg}" for tl in range(NT) for cg in range(2))
        wp = S["wproj"][:]
        for e in range(NE):
            for jg in range(4):
                for js in range(4):
                    j = jg * 4 + js
                    for u in range(2):
                        dstu = bass.AP(wp.tensor, wp.offset + js * 256 + u * 128,
                                       [[wp.ap[0][0], 128], [1024, NCH], [1, 128]])
                        srcu = dram["w13T"].rearrange("e (c p) f -> e p c f", p=128)[
                            e, :, :, u * FFN + j * 128: u * FFN + (j + 1) * 128]
                        self.ld(dstu, srcu, (f"w13s:{js}",), rkeys=(f"w13s:{js}",),
                                q="gpsimd", tag=f"mw13_{e}_{j}_{u}")
                    dst2 = wp[:, 8192 + js * 1024:8192 + (js + 1) * 1024]
                    self.ld(dst2, dram["w2T"][e, j * 128:(j + 1) * 128, :],
                            (f"w2s:{js}",), rkeys=(f"w2s:{js}",),
                            q="gpsimd", tag=f"mw2_{e}_{j}")
                    for blk in range(4):
                        pairs = [(wp[:, c * 1024 + js * 256: c * 1024 + js * 256 + 128],
                                  hfm[:, c * T + blk * 512: c * T + (blk + 1) * 512])
                                 for c in range(NCH)]
                        self.mmg(psA[:, blk * 512:(blk + 1) * 512], pairs,
                                 [f"w13s:{js}"] + list(hkeys),
                                 (f"psA:{blk}",), tag=f"m1_{e}_{j}_{blk}")
                    gsl = gbuf[:, js * 2048:(js + 1) * 2048]

                    def esl(eng, o=gsl):
                        return eng.activation(out=o, in_=psA[:, 0:2048], func=AF.Silu)
                    self.a(esl, r=("psA:0", "psA:1", "psA:2", "psA:3"),
                           w=(f"g:{js}",), tag=f"msl{e}_{j}")
                    for blk in range(4):
                        pairs = [(wp[:, c * 1024 + js * 256 + 128:
                                     c * 1024 + js * 256 + 256],
                                  hfm[:, c * T + blk * 512: c * T + (blk + 1) * 512])
                                 for c in range(NCH)]
                        self.mmg(psA[:, blk * 512:(blk + 1) * 512], pairs,
                                 [f"w13s:{js}", f"g:{js}"] + list(hkeys),
                                 (f"psA:{blk}",), tag=f"m3_{e}_{j}_{blk}")

                    def emu(eng, o=gsl):
                        return eng.tensor_tensor(out=o, in0=o, in1=psA[:, 0:2048],
                                                 op=ALU.mult)
                    self.v(emu, r=(f"g:{js}", "psA:0", "psA:1", "psA:2", "psA:3"),
                           w=(f"g:{js}",), tag=f"mmu{e}_{j}")
                for tl in range(NT):
                    for g in range(2):
                        pairs = []
                        for js in range(4):
                            gsl = gbuf[:, js * 2048:(js + 1) * 2048]
                            pairs.append(
                                (gsl[:, tl * 128:(tl + 1) * 128],
                                 wp[:, 8192 + js * 1024 + g * 512:
                                    8192 + js * 1024 + (g + 1) * 512]))
                        self.mmg(psB[:, g * 512:(g + 1) * 512], pairs,
                                 [f"g:{js2}" for js2 in range(4)] +
                                 [f"w2s:{js2}" for js2 in range(4)],
                                 (f"psB:{4 + g}",), tag=f"md{e}_{jg}_{tl}_{g}")
                    xt = S["xres"][:, tl * D:(tl + 1) * D]

                    def eacc(eng, x=xt, e=e, tl=tl):
                        return eng.scalar_tensor_tensor(
                            out=x, in0=psB[:, 0:1024],
                            scalar=S["wfull"][:, tl * NE + e:tl * NE + e + 1],
                            in1=x, op0=ALU.mult, op1=ALU.add)
                    self.v(eacc, r=("psB:4", "psB:5", f"xres:{tl}", f"wfull:{tl}"),
                           w=(f"xres:{tl}",), tag=f"ma{e}_{jg}_{tl}")

    def final_out(self, sq):
        S, dram = self.S, self.dram
        x1t = self.mvf("x1t", 1024)
        res = self.mvf("res", 1024)
        for tl in range(NT):
            self.ld(x1t, dram["x1scr"][tl * 128:(tl + 1) * 128, :],
                    ("x1t",), rkeys=("x1t", f"x1scr:{tl // TPS}"), tag=f"fx1{tl}")
            xt = S["xres"][:, tl * D:(tl + 1) * D]
            mk, mean, rk, rstd = self.stats(xt, (f"xres:{tl}",), "ln", tag=f"fo{tl}")

            def efin(eng, x=xt, m=mean, r=rstd):
                return eng.tensor_scalar(out=res, in0=x, scalar1=m, scalar2=r,
                                         op0=ALU.subtract, op1=ALU.mult)
            self.v(efin, r=(f"xres:{tl}",), sr=(mk, rk), w=("res",), tag=f"ff{tl}")

            def eadd(eng):
                return eng.tensor_tensor(out=res, in0=res, in1=x1t, op=ALU.add)
            self.v(eadd, r=("res", "x1t"), w=("res",), tag=f"fa{tl}")
            self.st(dram["y"][sq * T + tl * 128: sq * T + (tl + 1) * 128, :],
                    res, ("res",), wkeys=("res",), tag=f"fy{sq}_{tl}")


def _host_tables():
    inv = 1.0 / 10000.0 ** (np.arange(0, HD, 2, dtype=np.float32) / HD)
    fr = np.arange(T, dtype=np.float32)[:, None] * inv[None, :]
    cA, sA = np.cos(fr).astype(np.float32), np.sin(fr).astype(np.float32)
    cpA = np.concatenate([cA, cA], axis=1)
    spA = np.concatenate([-sA, sA], axis=1)
    invb = (np.float32(1.0) /
            np.float32(10000.0) ** (np.arange(0, HD, 2, dtype=np.float32) /
                                    np.float32(HD))).astype(np.float32)
    frb = (np.arange(T, dtype=np.float32)[:, None] * invb[None, :]).astype(np.float32)
    cB, sB = np.cos(frb).astype(np.float32), np.sin(frb).astype(np.float32)
    cpB = np.concatenate([cB, cB], axis=1)
    spB = np.concatenate([sB, -sB], axis=1)
    return cpA, spA, cpB, spB


_CACHE = {}


def make_common(inputs):
    import ml_dtypes
    f = lambda nm: np.ascontiguousarray(np.asarray(inputs[nm], dtype=np.float32))
    cpA, spA, cpB, spB = _host_tables()
    row, col = np.arange(128)[:, None], np.arange(128)[None, :]
    maskadd = np.where(col > row, np.float32(MASKVAL), np.float32(0.0)).astype(np.float32)
    common = {
        "Wq": f("L0_attn_Wq"), "Wk": f("L0_attn_Wk"), "Wv": f("L0_attn_Wv"),
        "Wo": f("L0_attn_Wo"),
        "qA": f("L0_attn_qA"), "kA": f("L0_attn_kA"), "vA": f("L0_attn_vA"),
        "pA": f("L0_attn_pA"),
        "qB": f("L0_attn_qB"), "kB": f("L0_attn_kB"), "vB": f("L0_attn_vB"),
        "pB": f("L0_attn_pB"),
        "q_gain": np.repeat(f("L0_attn_q_gain").reshape(H)[
            [(m % 2) * 8 + m // 2 for m in range(H)]], HD).reshape(1, D).copy(),
        "head_gate": f("L0_attn_head_gate").reshape(1, H),
        "ln1_w": f("L0_ffn_ln1_w").reshape(1, D),
        "ln2_w": f("L0_ffn_ln2_w").reshape(1, D),
        "wqkvT": np.ascontiguousarray(f("L0_ffn_wqkv").T),
        "wo2T": np.ascontiguousarray(f("L0_ffn_wo").T),
        "gateT": np.ascontiguousarray(f("L0_ffn_gate_w").T),
        "w13T": np.ascontiguousarray(f("L0_ffn_w13").transpose(0, 2, 1)),
        "w2T": np.ascontiguousarray(f("L0_ffn_w2").transpose(0, 2, 1)),
        "cpA": cpA, "spA": spA, "cpB": cpB, "spB": spB,
        "maskadd": maskadd,
        "identf": np.eye(128, dtype=np.float32),
        "identb": np.eye(128, dtype=np.float32).astype(ml_dtypes.bfloat16),
        "onesrow": np.ones((1, 128), dtype=np.float32),
    }
    return common


def kernel(**inputs):
    if "nc" not in _CACHE:
        kb = KB()
        _CACHE["nc"] = kb.build()
        _CACHE["kb"] = kb
    nc = _CACHE["nc"]
    x = np.ascontiguousarray(np.asarray(inputs["L0_attn_x"], dtype=np.float32))
    common = make_common(inputs)
    in_maps = []
    for c in range(8):
        m = dict(common)
        m["x"] = np.ascontiguousarray(x[c * NSEQ:(c + 1) * NSEQ].reshape(NSEQ * T, D))
        in_maps.append(m)
    res = run_bass_kernel_spmd(nc, in_maps, core_ids=list(range(8)))
    out = np.empty((B, T, D), dtype=np.float32)
    for c in range(8):
        out[c * NSEQ:(c + 1) * NSEQ] = np.asarray(res.results[c]["y"]).reshape(NSEQ, T, D)
    return out


# version 06
# speedup vs baseline: 1.2250x; 1.2250x over previous
import numpy as np
import concourse.bass as bass
import concourse.mybir as mybir
from concourse.bass_utils import run_bass_kernel_spmd
from contextlib import ExitStack

F32 = mybir.dt.float32
F16 = mybir.dt.float16
BF16 = mybir.dt.bfloat16
I32 = mybir.dt.int32
AF = mybir.ActivationFunctionType
ALU = mybir.AluOpType
AX = mybir.AxisListType

B, T, D = 32, 2048, 1024
H, KVH, HD = 16, 4, 64
RANK = 32
NE, FFN = 8, 2048
NSEQ, NT = 4, 16
SLAB = 256
TPS = SLAB // 128
NSLAB = T // SLAB
NCH = D // 128
QMAXF = 31.0
INV31 = float(np.float32(1.0) / np.float32(31.0))
SOFT_SCALE = 0.125
MASKVAL = -1e31


class Prog:
    ENGINES = ("sync", "gpsimd", "vector", "scalar", "tensor")
    DMAQ = {"sync": "dsync", "gpsimd": "dgp"}
    NPOOL = 6

    def __init__(self):
        self.ops = []
        self.writer = {}
        self.readers = {}
        self.bar = []

    def op(self, eng, emit, r=(), w=(), sr=(), dma=False, tag=""):
        deps = {}
        for b in r:
            j = self.writer.get(b)
            if j is not None:
                deps.setdefault(j, False)
        for b in sr:
            j = self.writer.get(b)
            if j is not None:
                deps[j] = True
        for b in w:
            j = self.writer.get(b)
            if j is not None:
                deps.setdefault(j, False)
            for jr in self.readers.get(b, ()):
                deps.setdefault(jr, False)
        for j in self.bar:
            deps.setdefault(j, False)
        i = len(self.ops)
        self.ops.append(dict(eng=eng, emit=emit, deps=deps, dma=dma, tag=tag,
                             milestone=False, sem=None, val=None, inc=0))
        for b in r:
            self.readers.setdefault(b, []).append(i)
        for b in sr:
            self.readers.setdefault(b, []).append(i)
        for b in w:
            self.writer[b] = i
            self.readers[b] = []
        return i

    def barrier(self):
        last = {}
        for i, o in enumerate(self.ops):
            last[o["eng"]] = i
        self.bar = list(last.values())

    def _wait_needed(self, i, j, scalar):
        return True

    def emit(self, nc, es, block):
        level = [0] * len(self.ops)
        for i, o in enumerate(self.ops):
            lv = 0
            for j in o["deps"]:
                if level[j] >= lv:
                    lv = level[j] + 1
            level[i] = lv
        order = sorted(range(len(self.ops)), key=lambda i: (level[i], i))
        remap = {old: new for new, old in enumerate(order)}
        self.ops = [self.ops[i] for i in order]
        for o in self.ops:
            o["deps"] = {remap[j]: sc for j, sc in o["deps"].items()}
        for i, o in enumerate(self.ops):
            for j, scalar in o["deps"].items():
                if self._wait_needed(i, j, scalar):
                    self.ops[j]["milestone"] = True
        for o in self.ops:
            if o["dma"]:
                o["milestone"] = True
        sems = {e: es.enter_context(nc.semaphore(f"se_{e}")) for e in self.ENGINES}
        dsems = {q: [es.enter_context(nc.semaphore(f"sd_{q}{k}"))
                     for k in range(self.NPOOL)] for q in self.DMAQ.values()}
        nc.all_engine_barrier()
        for s in list(sems.values()) + [s for p in dsems.values() for s in p]:
            nc.gpsimd.sem_clear(s)
        nc.all_engine_barrier()
        cnt = {e: 0 for e in self.ENGINES}
        dcnt = {q: 0 for q in self.DMAQ.values()}
        for o in self.ops:
            if not o["milestone"]:
                continue
            if o["dma"]:
                q = self.DMAQ[o["eng"]]
                k = dcnt[q]
                dcnt[q] += 1
                o["sem"] = dsems[q][k % self.NPOOL]
                o["val"] = 16 * (k // self.NPOOL + 1)
                o["inc"] = 16
                if k >= self.NPOOL:
                    o["self_wait"] = (o["sem"], 16 * (k // self.NPOOL))
            else:
                cnt[o["eng"]] += 1
                o["sem"], o["val"], o["inc"] = sems[o["eng"]], cnt[o["eng"]], 1
        marks = {e: {} for e in self.ENGINES}
        for i, o in enumerate(self.ops):
            waits = {}
            for j, scalar in o["deps"].items():
                if not self._wait_needed(i, j, scalar):
                    continue
                oj = self.ops[j]
                key = id(oj["sem"])
                if waits.get(key, (None, -1))[1] < oj["val"]:
                    waits[key] = (oj["sem"], oj["val"])
            if o.get("self_wait"):
                s, v = o["self_wait"]
                if waits.get(id(s), (None, -1))[1] < v:
                    waits[id(s)] = (s, v)
            m = marks[o["eng"]]
            o["waits"] = []
            for key, (s, v) in waits.items():
                if m.get(key, -1) >= v:
                    continue
                m[key] = v
                o["waits"].append((s, v))
        streams = {e: [o for o in self.ops if o["eng"] == e] for e in self.ENGINES}

        def mk(ops, is_sync):
            def body(eng):
                for o in ops:
                    for (s, v) in o["waits"]:
                        eng.wait_ge(s, v)
                    inst = o["emit"](eng)
                    if o["milestone"]:
                        assert inst is not None, o["tag"]
                        inst.then_inc(o["sem"], o["inc"])
                if is_sync:
                    for q, pool in dsems.items():
                        n = dcnt[q]
                        for k in range(min(self.NPOOL, n)):
                            eng.wait_ge(pool[k], 16 * ((n - k - 1) // self.NPOOL + 1))
                    for e in self.ENGINES:
                        if e != "sync" and cnt[e] > 0:
                            eng.wait_ge(sems[e], cnt[e])
            return body

        for e in self.ENGINES:
            if streams[e] or e == "sync":
                getattr(block, e)(mk(streams[e], e == "sync"))
        return {e: len(v) for e, v in streams.items()}


class KB:
    CV = dict(qfm=0, yfm=1024, P=2048, Pt=3072, rsA=5120, rsB=6144,
              qtm=7168, ltm=8192, ktm=9216, lfm=9472, WT=10496)
    MCV = dict(hfm=0, gbuf=8192, h2f=12288, htm=13312, hfmf=13824, res=14848, x1t=12288)
    FQ = dict(wt=0, y=1024, q0f=2048, t1=3072, q1=4096, q2=5120,
              d1=6144, d2=7168, qi=8192, asm=9216, wdq=13500)

    def __init__(self, debug_taps=()):
        self.debug_taps = dict(debug_taps)

    def sslot(self, n):
        c = self._sc
        if c + n > 64:
            c = 0
        self._sc = c + n
        return c

    def skey(self, c, i=0):
        return f"scal:{c + i}"

    def sap(self, c, i=0):
        return self.S["scal"][:, c + i:c + i + 1]

    def ld(self, dst, src, wkeys, rkeys=(), q="sync", tag="ld"):
        self.P.op(q, (lambda eng, d=dst, s=src: eng.dma_start(out=d, in_=s)),
                  r=tuple(rkeys), w=tuple(wkeys), dma=True, tag=tag)

    def st(self, dst, src, rkeys, wkeys=(), q="sync", tag="st"):
        self.P.op(q, (lambda eng, d=dst, s=src: eng.dma_start(out=d, in_=s)),
                  r=tuple(rkeys), w=tuple(wkeys), dma=True, tag=tag)

    def v(self, emit, r=(), w=(), sr=(), tag=""):
        self.P.op("vector", emit, r=r, w=w, sr=sr, tag=tag)

    def a(self, emit, r=(), w=(), sr=(), tag=""):
        self.P.op("scalar", emit, r=r, w=w, sr=sr, tag=tag)

    def t(self, emit, r=(), w=(), sr=(), tag=""):
        self.P.op("tensor", emit, r=r, w=w, sr=sr, tag=tag)

    def mmg(self, psum_ap, pairs, rkeys, wkeys, tag="mm"):
        def em(eng, p=psum_ap, prs=tuple(pairs)):
            inst = None
            n = len(prs)
            for i, (l, rr) in enumerate(prs):
                inst = eng.matmul(out=p, lhsT=l, rhs=rr, start=(i == 0),
                                  stop=(i == n - 1))
            return inst
        self.t(em, r=tuple(rkeys), w=tuple(wkeys), tag=tag)

    def cvf(self, nm, n):
        return self.ovl[:, self.CV[nm]:self.CV[nm] + n]

    def cvb(self, nm, n):
        return self.ovb[:, 2 * self.CV[nm]:2 * self.CV[nm] + n]

    def mvf(self, nm, n):
        return self.ovl[:, self.MCV[nm]:self.MCV[nm] + n]

    def mvb(self, nm, n):
        return self.ovb[:, 2 * self.MCV[nm]:2 * self.MCV[nm] + n]

    def stats(self, src_ap, rkeys, kind, tag="st"):
        S = self.S
        bslot = self._bn
        self._bn ^= 1
        bn = S["bn"][:, bslot * 12:(bslot + 1) * 12]
        bnk = f"bn:{bslot}"
        c = self.sslot(4)

        def e1(eng, a=src_ap, b=bn):
            eng.bn_stats(out=b[:, 0:6], in_=a[:, 0:512])
            return eng.bn_stats(out=b[:, 6:12], in_=a[:, 512:1024])
        self.v(e1, r=tuple(rkeys), w=(bnk,), tag=tag + "bn")

        def e2(eng, b=bn, o=S["scal"][:, c:c + 2]):
            return eng.bn_aggr(out=o, in_=b)
        self.v(e2, r=(bnk,), w=(self.skey(c), self.skey(c, 1)), tag=tag + "ag")
        mean, var = self.sap(c), self.sap(c, 1)
        if kind == "rms":
            o2 = self.sap(c, 2)

            def e3(eng, m=mean, o=o2):
                return eng.tensor_tensor(out=o, in0=m, in1=m, op=ALU.mult)
            self.v(e3, sr=(self.skey(c),), w=(self.skey(c, 2),), tag=tag + "m2")

            def e4(eng, vv=var, o=o2):
                return eng.tensor_tensor(out=o, in0=o, in1=vv, op=ALU.add)
            self.v(e4, sr=(self.skey(c, 1), self.skey(c, 2)), w=(self.skey(c, 2),),
                   tag=tag + "ms")
            vsrc, vk = o2, self.skey(c, 2)
        else:
            vsrc, vk = var, self.skey(c, 1)
        rstd = self.sap(c, 3)

        def e4e(eng, i=vsrc):
            return eng.tensor_single_scalar(out=i, in_=i, scalar=1e-5, op=ALU.add)
        self.v(e4e, sr=(vk,), w=(vk,), tag=tag + "ep")

        def e5(eng, i=vsrc, o=rstd):
            return eng.activation(out=o, in_=i, func=AF.Ln)
        self.a(e5, sr=(vk,), w=(self.skey(c, 3),), tag=tag + "ln")

        def e6(eng, o=rstd):
            return eng.activation(out=o, in_=o, func=AF.Exp, scale=-0.5)
        self.a(e6, sr=(self.skey(c, 3),), w=(self.skey(c, 3),), tag=tag + "ex")
        return self.skey(c), mean, self.skey(c, 3), rstd

    def tab(self, name, tile, reps):
        tt = self.S["s_" + name][:]
        return bass.AP(tt.tensor, tt.offset + tile * HD,
                       [[tt.ap[0][0], 128], [0, reps], [1, HD]])

    def tab4(self, name, tile, reps):
        tt = self.S["s_" + name][:]
        return bass.AP(tt.tensor, tt.offset + tile * HD,
                       [[tt.ap[0][0], 128], [0, reps], [32, 2], [1, 32]])

    def shufw(self, ap, nheads):
        return bass.AP(ap.tensor, ap.offset + 32,
                       [[ap.ap[0][0], 128], [64, nheads], [-32, 2], [1, 32]])

    def g64(self, ap, nheads):
        return bass.AP(ap.tensor, ap.offset,
                       [[ap.ap[0][0], 128], [64, nheads], [32, 2], [1, 32]])

    def build(self):
        nc = bass.Bass("TRN2", target_bir_lowering=False)
        self.nc = nc
        self.P = Prog()
        self._sc = 0
        self._bn = 0

        dram = {}
        specs = [
            ("x", [NSEQ * T, D], F32),
            ("Wq", [D, D], F32), ("Wk", [KVH * HD, D], F32),
            ("Wv", [KVH * HD, D], F32), ("Wo", [D, D], F32),
            ("qA", [D, RANK], F32), ("kA", [D, RANK], F32), ("vA", [D, RANK], F32),
            ("pA", [D, RANK], F32),
            ("qB", [RANK, D], F32), ("kB", [RANK, KVH * HD], F32),
            ("vB", [RANK, KVH * HD], F32), ("pB", [RANK, D], F32),
            ("q_gain", [1, D], F32), ("head_gate", [1, H], F32),
            ("ln1_w", [1, D], F32), ("ln2_w", [1, D], F32),
            ("wqkvT", [D, (H + 2 * KVH) * HD], F32), ("wo2T", [D, H * HD], F32),
            ("gateT", [D, NE], F32),
            ("w13T", [NE, D, 2 * FFN], F32), ("w2T", [NE, FFN, D], F32),
            ("cpA", [T, HD], F32), ("spA", [T, HD], F32),
            ("cpB", [T, HD], F32), ("spB", [T, HD], F32),
            ("maskadd", [128, 128], F32), ("identf", [128, 128], F32),
            ("onesrow", [1, 128], F32),
        ]
        for nm, shp, dt in specs:
            dram[nm] = nc.dram_tensor(nm, shp, dt, kind="ExternalInput").ap()
        dram["identb"] = nc.dram_tensor("identb", [128, 128], BF16,
                                        kind="ExternalInput").ap()
        dram["y"] = nc.dram_tensor("y", [NSEQ * T, D], F32, kind="ExternalOutput").ap()
        for nm, wid in [("wsq", D), ("wsk", KVH * HD), ("wsv", KVH * HD), ("wso", D)]:
            dram[nm] = nc.dram_tensor(nm, [D, wid], BF16, kind="Internal").ap()
        dram["x1scr"] = nc.dram_tensor("x1scr", [T, D], F32, kind="Internal").ap()
        taps = {}
        for nm, (shp, dt) in self.debug_taps.items():
            taps[nm] = nc.dram_tensor(nm, shp, dt, kind="ExternalOutput").ap()
        self.dram, self.taps = dram, taps

        es = ExitStack()
        with es:
            S = {}
            self.S = S

            def sb(name, shape, dt):
                S[name] = es.enter_context(nc.sbuf_tensor(name, shape, dt))

            sb("xres", [128, NT * D], F32)
            sb("kfm", [128, 2 * T], BF16)
            sb("vtm", [128, NT * KVH * HD], BF16)
            sb("wproj", [128, NCH * 1536], BF16)
            sb("amats", [128, NCH * 128], BF16)
            sb("bmats", [RANK, 2560], BF16)
            sb("usb", [RANK, 3 * 128], BF16)
            sb("scal", [128, 64], F32)
            sb("bn", [128, 32], F32)
            sb("s_cpA", [128, NT * HD], F32)
            sb("s_spA", [128, NT * HD], F32)
            sb("s_cpB", [128, NT * HD], F32)
            sb("s_spB", [128, NT * HD], F32)
            sb("gainb", [128, D], BF16)
            sb("gateb", [128, H], F32)
            sb("w1b", [128, D], BF16)
            sb("w2b", [128, D], BF16)
            sb("s_maskadd", [128, 128], F32)
            sb("s_identb", [128, 128], BF16)
            sb("s_identf", [128, 128], F32)
            sb("s_onesrow", [1, 128], F32)
            sb("gatef", [128, NCH * NE], F32)
            sb("wfull", [128, NT * NE], F32)
            sb("rowtmp", [1, D], F32)
            sb("ovl", [128, 16000], F32)
            if "tap_fqdbg" in self.debug_taps:
                sb("fqdbg", [128, 64], F32)
            self.ovl = S["ovl"]
            self.ovb = S["ovl"].bitcast(BF16)
            self.ovi = S["ovl"].bitcast(I32)

            self.psA = es.enter_context(nc.psum_tensor("psA", [128, 2048], F32))
            self.psB = es.enter_context(nc.psum_tensor("psB", [128, 2048], F32))
            self.psBb = self.psB.bitcast(BF16)

            block = es.enter_context(nc.Block())

            self.fakequant("Wq", D, "wsq")
            self.setup_consts()
            self.P.barrier()
            for wnm, nrows, scr in [("Wq", D, "wsq"), ("Wk", KVH * HD, "wsk"),
                                    ("Wv", KVH * HD, "wsv"), ("Wo", D, "wso")]:
                self.fakequant(wnm, nrows, scr)
                if f"tap_{scr}" in self.taps:
                    self.st(self.taps[f"tap_{scr}"], dram[scr], (scr,),
                            tag=f"tap{scr}")
                if "tap_fqdbg" in self.taps and wnm == "Wq":
                    self.st(self.taps["tap_fqdbg"], self.S["fqdbg"][:],
                            tuple(f"dbg:{tl}{x}" for tl in range(8) for x in "abc"),
                            tag="tapfqdbg")
            self.load_small_weights()
            self.P.barrier()
            for sq in range(NSEQ):
                self.sequence(sq)
            self.n_inst = self.P.emit(nc, es, block)
        return nc

    def setup_consts(self):
        S, dram = self.S, self.dram
        for nm in ("maskadd", "identb", "identf", "onesrow"):
            self.ld(S["s_" + nm][:], dram[nm], (nm,), tag="c_" + nm)
        for nm in ("cpA", "spA", "cpB", "spB"):
            self.ld(S["s_" + nm][:], dram[nm].rearrange("(t p) f -> p t f", p=128),
                    (nm,), tag="c_" + nm)
        self.ld(S["gatef"][:].rearrange("p (c e) -> p c e", e=NE),
                dram["gateT"].rearrange("(c p) e -> p c e", p=128),
                ("gatef",), tag="c_gate")

        def bcast(src_dram, width, rep_each, dst, tag):
            rt = S["rowtmp"]
            self.ld(rt[0:1, 0:width], src_dram, ("rowtmp",), rkeys=("rowtmp",),
                    tag="cb_" + tag)
            wr = width * rep_each
            if rep_each > 1:
                rta = rt[:]
                src = bass.AP(rta.tensor, rta.offset,
                              [[rta.ap[0][0], 1], [1, width], [0, rep_each]])

                def er(eng, s=src,
                       o=rt[0:1, 0:wr].rearrange("p (a b) -> p a b", a=width)):
                    return eng.tensor_copy(out=o, in_=s)
                self.v(er, r=("rowtmp",), w=("rowtmp",), tag="cbr_" + tag)
                sk = "rowtmp"
            else:
                sk = "rowtmp"
            for hh in range((wr + 511) // 512):
                w0, w1 = hh * 512, min(wr, hh * 512 + 512)

                def em(eng, w0=w0, w1=w1, rt=rt):
                    return eng.matmul(out=self.psB[:, 0:w1 - w0],
                                      lhsT=S["s_onesrow"][0:1, :], rhs=rt[0:1, w0:w1],
                                      start=True, stop=True)
                self.t(em, r=(sk, "onesrow"), w=("psB:4",), tag="cbm_" + tag)

                def ee(eng, w0=w0, w1=w1, d=S[dst]):
                    return eng.tensor_copy(out=d[:, w0:w1], in_=self.psB[:, 0:w1 - w0])
                self.v(ee, r=("psB:4",), w=(dst,), tag="cbe_" + tag)

        bcast(self.dram["q_gain"], D, 1, "gainb", "qg")
        bcast(self.dram["head_gate"], H, 1, "gateb", "hg")
        bcast(self.dram["ln1_w"], D, 1, "w1b", "w1")
        bcast(self.dram["ln2_w"], D, 1, "w2b", "w2")

    def load_small_weights(self):
        S, dram = self.S, self.dram
        am = S["amats"][:]
        for i, nm in enumerate(("qA", "kA", "vA", "pA")):
            dst = bass.AP(am.tensor, am.offset + i * 32,
                          [[am.ap[0][0], 128], [128, NCH], [1, 32]])
            self.ld(dst, dram[nm].rearrange("(c p) r -> p c r", p=128),
                    (f"am_{nm}",), q="gpsimd", tag="w_" + nm)
        for nm, o in (("qB", 0), ("kB", 1024), ("vB", 1280), ("pB", 1536)):
            wdt = dram[nm].shape[1]
            self.ld(S["bmats"][0:RANK, o:o + wdt], dram[nm], (f"bm_{nm}",),
                    q="gpsimd", tag="w_" + nm)
        self.AMK = ("am_qA", "am_kA", "am_vA", "am_pA")

    def fakequant(self, Wnm, nrows, scr):
        S, dram, ovl, ovb, ovi = self.S, self.dram, self.ovl, self.ovb, self.ovi
        FQ = self.FQ
        ntl = nrows // 128
        wt = ovl[:, FQ["wt"]:FQ["wt"] + 1024]
        y = ovl[:, FQ["y"]:FQ["y"] + 1024]
        q0f = ovl[:, FQ["q0f"]:FQ["q0f"] + 1024]
        t1 = ovl[:, FQ["t1"]:FQ["t1"] + 1024]
        qi = ovi[:, FQ["qi"]:FQ["qi"] + 1024]
        q1b = ovl[:, FQ["q1"]:FQ["q1"] + 1024]
        q2b = ovl[:, FQ["q2"]:FQ["q2"] + 1024]
        d1b = ovl[:, FQ["d1"]:FQ["d1"] + 1024]
        d2b = ovl[:, FQ["d2"]:FQ["d2"] + 1024]
        wdq = ovb[:, 2 * FQ["wdq"]:2 * FQ["wdq"] + 1024]
        asm = ovb[:, 2 * FQ["asm"]:2 * FQ["asm"] + NCH * nrows]
        for tl in range(ntl):
            self.ld(wt, dram[Wnm].rearrange("(n p) d -> n p d", p=128)[tl],
                    ("fq_wt",), rkeys=("fq_wt",), tag=f"fq_{Wnm}{tl}")
            bslot = self._bn
            self._bn ^= 1
            bn = S["bn"][:, bslot * 12:(bslot + 1) * 12]
            bnk = f"bn:{bslot}"

            def ebn(eng, b=bn):
                eng.bn_stats(out=b[:, 0:6], in_=wt[:, 0:512])
                return eng.bn_stats(out=b[:, 6:12], in_=wt[:, 512:1024])
            self.v(ebn, r=("fq_wt",), w=(bnk,), tag="fqbn")
            c = self.sslot(12)
            sc = S["scal"]
            K = lambda i: self.skey(c, i)
            var = self.sap(c, 1)
            std, am_ = self.sap(c, 2), self.sap(c, 3)
            ta, tb = self.sap(c, 4), self.sap(c, 5)
            s1, s2 = self.sap(c, 6), self.sap(c, 7)
            r1, r2 = self.sap(c, 8), self.sap(c, 9)
            e1, e2 = self.sap(c, 10), self.sap(c, 11)

            def eag(eng, b=bn, o=sc[:, c:c + 2]):
                return eng.bn_aggr(out=o, in_=b)
            self.v(eag, r=(bnk,), w=(K(0), K(1)), tag="fqag")

            def eln(eng, i=var, o=std):
                return eng.activation(out=o, in_=i, func=AF.Ln)
            self.a(eln, sr=(K(1),), w=(K(2),), tag="fqln")

            def eex(eng, o=std):
                return eng.activation(out=o, in_=o, func=AF.Exp, scale=0.5)
            self.a(eex, sr=(K(2),), w=(K(2),), tag="fqex")

            def eam(eng, o=am_):
                return eng.reduce_max(out=o, in_=wt, axis=AX.X,
                                      apply_absolute_value=True)
            self.v(eam, r=("fq_wt",), w=(K(3),), tag="fqam")

            def ea(eng, i=std, o=ta):
                return eng.tensor_scalar(out=o, in0=i, scalar1=1e-8, scalar2=12.85,
                                         op0=ALU.max, op1=ALU.mult)
            self.v(ea, sr=(K(2),), w=(K(4),), tag="fqa")

            def eb(eng, i=am_, o=tb):
                return eng.tensor_scalar(out=o, in0=i, scalar1=INV31, scalar2=INV31,
                                         op0=ALU.max, op1=ALU.mult)
            self.v(eb, sr=(K(3),), w=(K(5),), tag="fqb")

            def es2(eng, i=tb, o=s2):
                return eng.tensor_single_scalar(out=o, in_=i, scalar=INV31, op=ALU.max)
            self.v(es2, sr=(K(5),), w=(K(7),), tag="fqs2")

            def ec(eng, o=ta, i=tb):
                return eng.tensor_tensor(out=o, in0=o, in1=i, op=ALU.max)
            self.v(ec, sr=(K(4), K(5)), w=(K(4),), tag="fqc")

            def es1(eng, i=ta, o=s1):
                return eng.tensor_scalar(out=o, in0=i, scalar1=INV31, scalar2=INV31,
                                         op0=ALU.mult, op1=ALU.max)
            self.v(es1, sr=(K(4),), w=(K(6),), tag="fqs1")

            def er1(eng, i=s1, o=r1):
                return eng.reciprocal(out=o, in_=i)
            self.v(er1, sr=(K(6),), w=(K(8),), tag="fqr1")

            def er2(eng, i=s2, o=r2):
                return eng.reciprocal(out=o, in_=i)
            self.v(er2, sr=(K(7),), w=(K(9),), tag="fqr2")

            def quant(rap, rkey, qdst, qkey):
                def ey(eng, rr=rap):
                    return eng.tensor_single_scalar(out=y, in_=wt, scalar=rr, op=ALU.mult)
                self.v(ey, r=("fq_wt",), sr=(rkey,), w=("fq_y",), tag="fqy")

                def eci(eng):
                    return eng.tensor_copy(out=qi, in_=y)
                self.v(eci, r=("fq_y",), w=("fq_qi",), tag="fqci")

                def ecf(eng):
                    return eng.tensor_copy(out=q0f, in_=qi)
                self.v(ecf, r=("fq_qi",), w=("fq_q0",), tag="fqcf")

                def ed(eng):
                    return eng.tensor_tensor(out=y, in0=y, in1=q0f, op=ALU.subtract)
                self.v(ed, r=("fq_y", "fq_q0"), w=("fq_y",), tag="fqd")

                def egt(eng):
                    return eng.tensor_single_scalar(out=t1, in_=y, scalar=0.5,
                                                    op=ALU.is_gt)
                self.v(egt, r=("fq_y",), w=("fq_t1",), tag="fqgt")

                def eadd(eng):
                    return eng.tensor_tensor(out=q0f, in0=q0f, in1=t1, op=ALU.add)
                self.v(eadd, r=("fq_q0", "fq_t1"), w=("fq_q0",), tag="fqad")

                def elt(eng):
                    return eng.tensor_single_scalar(out=t1, in_=y, scalar=-0.5,
                                                    op=ALU.is_lt)
                self.v(elt, r=("fq_y",), w=("fq_t1",), tag="fqlt")

                def esub(eng):
                    return eng.tensor_tensor(out=q0f, in0=q0f, in1=t1, op=ALU.subtract)
                self.v(esub, r=("fq_q0", "fq_t1"), w=("fq_q0",), tag="fqsb")

                def ecl(eng, o=qdst):
                    return eng.tensor_scalar(out=o, in0=q0f, scalar1=QMAXF,
                                             scalar2=-QMAXF, op0=ALU.min, op1=ALU.max)
                self.v(ecl, r=("fq_q0",), w=(qkey,), tag="fqcl")

            quant(r1, K(8), q1b, "fq_q1")
            quant(r2, K(9), q2b, "fq_q2")

            def ed1(eng):
                return eng.scalar_tensor_tensor(out=d1b, in0=q1b, scalar=s1, in1=wt,
                                                op0=ALU.mult, op1=ALU.subtract)
            self.v(ed1, r=("fq_q1", "fq_wt"), sr=(K(6),), w=("fq_d1",), tag="fqd1")

            def ee1a(eng, b=bn):
                eng.bn_stats(out=b[:, 0:6], in_=d1b[:, 0:512])
                return eng.bn_stats(out=b[:, 6:12], in_=d1b[:, 512:1024])
            self.v(ee1a, r=("fq_d1",), w=(bnk,), tag="fqe1a")

            c1x = self.sslot(2)

            def ee1b(eng, b=bn, o=sc[:, c1x:c1x + 2]):
                return eng.bn_aggr(out=o, in_=b)
            self.v(ee1b, r=(bnk,), w=(self.skey(c1x), self.skey(c1x, 1)), tag="fqe1b")

            def ee1c(eng):
                return eng.scalar_tensor_tensor(out=e1, in0=self.sap(c1x),
                                                scalar=self.sap(c1x),
                                                in1=self.sap(c1x, 1),
                                                op0=ALU.mult, op1=ALU.add)
            self.v(ee1c, sr=(self.skey(c1x), self.skey(c1x, 1)), w=(K(10),),
                   tag="fqe1c")

            def ed2(eng):
                return eng.scalar_tensor_tensor(out=d2b, in0=q2b, scalar=s2, in1=wt,
                                                op0=ALU.mult, op1=ALU.subtract)
            self.v(ed2, r=("fq_q2", "fq_wt"), sr=(K(7),), w=("fq_d2",), tag="fqd2")

            bslot2 = self._bn
            self._bn ^= 1
            bn2 = S["bn"][:, bslot2 * 12:(bslot2 + 1) * 12]
            bnk2 = f"bn:{bslot2}"
            c2 = self.sslot(2)
            ea2, eb2 = self.sap(c2), self.sap(c2, 1)

            def ee2a(eng, b=bn2):
                eng.bn_stats(out=b[:, 0:6], in_=d2b[:, 0:512])
                return eng.bn_stats(out=b[:, 6:12], in_=d2b[:, 512:1024])
            self.v(ee2a, r=("fq_d2",), w=(bnk2,), tag="fqe2a")

            def ee2b(eng, b=bn2, o=sc[:, c2:c2 + 2]):
                return eng.bn_aggr(out=o, in_=b)
            self.v(ee2b, r=(bnk2,), w=(self.skey(c2), self.skey(c2, 1)), tag="fqe2b")

            def ee2c(eng):
                return eng.scalar_tensor_tensor(out=e2, in0=ea2, scalar=ea2, in1=eb2,
                                                op0=ALU.mult, op1=ALU.add)
            self.v(ee2c, sr=(self.skey(c2), self.skey(c2, 1)), w=(K(11),), tag="fqe2c")

            use1 = ta

            def eu(eng, o=use1):
                return eng.tensor_tensor(out=o, in0=e1, in1=e2, op=ALU.is_lt)
            self.v(eu, sr=(K(10), K(11)), w=(K(4),), tag="fqu")
            if "tap_fqdbg" in self.taps and Wnm == "Wq":
                def edbgu(eng, o=S["fqdbg"][:, tl * 8 + 6:tl * 8 + 8],
                          i=sc[:, c + 10:c + 12]):
                    return eng.tensor_copy(out=o, in_=i)
                self.v(edbgu, sr=(K(10), K(11)), w=(f"dbg:{tl}c",), tag="dbgu")

            if "tap_fqdbg" in self.taps and Wnm == "Wq":
                def edbgpre(eng, o=S["fqdbg"][:, tl * 8:tl * 8 + 4],
                            i=sc[:, c + 4:c + 8]):
                    return eng.tensor_copy(out=o, in_=i)
                self.v(edbgpre, sr=(K(4), K(5), K(6), K(7)), w=(f"dbg:{tl}a",),
                       tag="dbgpre")
            s16 = S["bn"].bitcast(F16)
            s12 = sc[:, c + 6:c + 8]

            def e1h(eng, o=s16[:, 48:50], i=s12):
                return eng.tensor_copy(out=o, in_=i)
            self.v(e1h, sr=(K(6), K(7)), w=("fq16:48",), tag="fqh")

            def e2h(eng, o=s12, i=s16[:, 48:50]):
                return eng.tensor_copy(out=o, in_=i)
            self.v(e2h, sr=("fq16:48",), w=(K(6), K(7)), tag="fqhb")
            if "tap_fqdbg" in self.taps and Wnm == "Wq":
                def edbgpost(eng, o=S["fqdbg"][:, tl * 8 + 4:tl * 8 + 6],
                             i=s12):
                    return eng.tensor_copy(out=o, in_=i)
                self.v(edbgpost, sr=(K(6), K(7)), w=(f"dbg:{tl}b",), tag="dbgpost")

            def edh1(eng):
                return eng.tensor_single_scalar(out=d1b, in_=q1b, scalar=s1, op=ALU.mult)
            self.v(edh1, r=("fq_q1",), sr=(K(6),), w=("fq_d1",), tag="fqD1")

            def edh2(eng):
                return eng.tensor_single_scalar(out=d2b, in_=q2b, scalar=s2, op=ALU.mult)
            self.v(edh2, r=("fq_q2",), sr=(K(7),), w=("fq_d2",), tag="fqD2")

            def edd(eng):
                return eng.tensor_tensor(out=d1b, in0=d1b, in1=d2b, op=ALU.subtract)
            self.v(edd, r=("fq_d1", "fq_d2"), w=("fq_d1",), tag="fqdd")

            def ew(eng):
                return eng.scalar_tensor_tensor(out=wdq, in0=d1b, scalar=use1, in1=d2b,
                                                op0=ALU.mult, op1=ALU.add)
            self.v(ew, r=("fq_d1", "fq_d2"), sr=(K(4),), w=("fq_wdq",), tag="fqw")

            for cg in range(2):
                def etr(eng, cg=cg):
                    inst = None
                    for c4 in range(4):
                        cc = cg * 4 + c4
                        inst = eng.transpose(
                            out=self.psBb[:, 1024 + c4 * 128:1024 + (c4 + 1) * 128],
                            in_=wdq[:, cc * 128:(cc + 1) * 128],
                            identity=S["s_identb"][:])
                    return inst
                self.t(etr, r=("fq_wdq", "identb"), w=("psB:5",), tag="fqtr")

                def ete(eng, cg=cg, tl=tl):
                    dst = bass.AP(asm.tensor, asm.offset + (cg * 4) * nrows + tl * 128,
                                  [[asm.ap[0][0], 128], [nrows, 4], [1, 128]])
                    return eng.tensor_copy(
                        out=dst,
                        in_=self.psBb[:, 1024:1536].rearrange("p (c f) -> p c f", c=4))
                self.v(ete, r=("psB:5",), w=(f"fqA:{tl}:{cg}",), tag="fqte")

        self.st(dram[scr].rearrange("(c p) f -> p c f", p=128),
                bass.AP(asm.tensor, asm.offset,
                        [[asm.ap[0][0], 128], [nrows, NCH], [1, nrows]]),
                tuple(f"fqA:{tl}:{cg}" for tl in range(ntl) for cg in range(2)),
                wkeys=(scr,), tag=f"fqsp{Wnm}")
        self.P.barrier()

    def sequence(self, sq):
        S, dram = self.S, self.dram
        self.phase_attn(sq, "A")
        if "tap_vA" in self.taps and sq == 0:
            self.st(self.taps["tap_vA"], S["vtm"][:],
                    tuple(f"vtm:{tl}" for tl in range(NT)), tag="tapvA")
        if "tap_kA" in self.taps and sq == 0:
            self.st(self.taps["tap_kA"], S["kfm"][:],
                    tuple(f"kfm:{tl}" for tl in range(NT)), tag="tapkA")
        for sl in range(NSLAB):
            self.st(dram["x1scr"][sl * SLAB:(sl + 1) * SLAB, :]
                    .rearrange("(a p) d -> p a d", p=128),
                    S["xres"][:, sl * TPS * D:(sl + 1) * TPS * D]
                    .rearrange("p (a d) -> p a d", d=D),
                    tuple(f"xres:{sl * TPS + i}" for i in range(TPS)),
                    wkeys=(f"x1scr:{sl}",), tag=f"spill{sq}_{sl}")
        if "tap_x1" in self.taps and sq == 0:
            for sl in range(NSLAB):
                self.st(self.taps["tap_x1"][sl * SLAB:(sl + 1) * SLAB, :]
                        .rearrange("(a p) d -> p a d", p=128),
                        S["xres"][:, sl * TPS * D:(sl + 1) * TPS * D]
                        .rearrange("p (a d) -> p a d", d=D),
                        tuple(f"xres:{sl * TPS + i}" for i in range(TPS)),
                        tag="tapx1")
        for tl in range(NT):
            xt = S["xres"][:, tl * D:(tl + 1) * D]
            mk, mean, rk, rstd = self.stats(xt, (f"xres:{tl}",), "ln", tag=f"lnB{tl}")

            def eap(eng, x=xt, m=mean, r=rstd):
                return eng.tensor_scalar(out=x, in0=x, scalar1=m, scalar2=r,
                                         op0=ALU.subtract, op1=ALU.mult)
            self.v(eap, r=(f"xres:{tl}", f"x1scr:{tl // TPS}"), sr=(mk, rk),
                   w=(f"xres:{tl}",), tag=f"lnBa{tl}")
        if "tap_fin" in self.taps and sq == 0:
            self.tap_xres("tap_fin")
        self.P.barrier()
        self.phase_attn(sq, "B")
        if "tap_u2" in self.taps and sq == 0:
            self.tap_xres("tap_u2")
        self.P.barrier()
        self.router_and_h2(sq)
        if "tap_wfull" in self.taps and sq == 0:
            self.st(self.taps["tap_wfull"], self.S["wfull"][:],
                    tuple(f"wfull:{tl}" for tl in range(NT)), tag="tapwf")
        if "tap_h2" in self.taps and sq == 0:
            hfm = self.mvb("hfm", NCH * T)
            self.st(self.taps["tap_h2"],
                    bass.AP(hfm.tensor, hfm.offset,
                            [[hfm.ap[0][0], 128], [T, NCH], [1, T]]),
                    tuple(f"hfm:{tl}:{cg}" for tl in range(NT) for cg in range(2)),
                    tag="taph2")
        self.P.barrier()
        self.moe(sq)
        if "tap_fout" in self.taps and sq == 0:
            self.tap_xres("tap_fout")
        self.P.barrier()
        self.final_out(sq)
        self.P.barrier()

    def tap_xres(self, nm):
        S = self.S
        for sl in range(NSLAB):
            self.st(self.taps[nm][sl * SLAB:(sl + 1) * SLAB, :]
                    .rearrange("(a p) d -> p a d", p=128),
                    S["xres"][:, sl * TPS * D:(sl + 1) * TPS * D]
                    .rearrange("p (a d) -> p a d", d=D),
                    tuple(f"xres:{sl * TPS + i}" for i in range(TPS)),
                    tag=f"tx{nm}{sl}")

    def phase_attn(self, sq, phase):
        S, dram = self.S, self.dram
        self._sq = sq
        A = phase == "A"
        wp = S["wproj"][:]
        wT = self.cvb("WT", NCH * 1024)
        if A:
            for nm, off, wid in (("wsq", 0, 1024), ("wsk", 1024, 256),
                                 ("wsv", 1280, 256)):
                dst = bass.AP(wp.tensor, wp.offset + off,
                              [[wp.ap[0][0], 128], [1536, NCH], [1, wid]])
                self.ld(dst, dram[nm].rearrange("(c p) f -> p c f", p=128),
                        ("wproj",), rkeys=(nm, "wproj"), tag=f"lw{nm}{sq}")
            dstT = bass.AP(wT.tensor, wT.offset,
                           [[wT.ap[0][0], 128], [1024, NCH], [1, 1024]])
            self.ld(dstT, dram["wso"].rearrange("(c p) f -> p c f", p=128),
                    ("WT",), rkeys=("wso", "WT"), tag=f"lwso{sq}")
            cpt, spt = "cpA", "spA"
        else:
            dst = bass.AP(wp.tensor, wp.offset,
                          [[wp.ap[0][0], 128], [1536, NCH], [1, 1536]])
            self.ld(dst, dram["wqkvT"].rearrange("(c p) f -> p c f", p=128),
                    ("wproj",), rkeys=("wproj",), q="gpsimd", tag=f"lwqkv{sq}")
            dstT = bass.AP(wT.tensor, wT.offset,
                           [[wT.ap[0][0], 128], [1024, NCH], [1, 1024]])
            self.ld(dstT, dram["wo2T"].rearrange("(c p) f -> p c f", p=128),
                    ("WT",), rkeys=("WT",), q="gpsimd", tag=f"lwo2{sq}")
            cpt, spt = "cpB", "spB"

        if A:
            for sl in range(NSLAB):
                keys = tuple(f"xres:{sl * TPS + i}" for i in range(TPS))
                self.ld(S["xres"][:, sl * TPS * D:(sl + 1) * TPS * D]
                        .rearrange("p (a d) -> p a d", d=D),
                        dram["x"][sq * T + sl * SLAB: sq * T + (sl + 1) * SLAB, :]
                        .rearrange("(a p) d -> p a d", p=128),
                        keys, rkeys=keys, tag=f"lx{sq}_{sl}")

        qfm = self.cvb("qfm", 2048)
        yfm = self.cvb("yfm", 2048)
        Pb = self.cvb("P", 2048)
        Ptb = self.cvb("Pt", 4096)
        for sl in range(NSLAB):
            for ti in range(TPS):
                self.qkv_tile(phase, sl, ti, qfm, cpt, spt)
            if ("tap_qf" in self.taps and A and sl == 4 and self._sq == 0):
                self.st(self.taps["tap_qf"], qfm,
                        tuple(f"qfm:{t2}:{cg}" for t2 in range(TPS)
                              for cg in range(2)), tag="tapqf")
            nkz = (sl + 1) * TPS

            def ez(eng, o=Ptb[:, (nkz - 1) * 256:(nkz - 1) * 256 + 128]):
                return eng.tensor_single_scalar(out=o, in_=S["s_identb"][:],
                                                scalar=0.0, op=ALU.mult)
            self.v(ez, r=("identb",), w=(f"Pt:{nkz - 1}",), tag=f"{phase}Pz{sl}")
            for h in range(H):
                self.attn_head(phase, sl, h, qfm, yfm, Pb, Ptb)
            if "tap_yf" in self.taps and sq == 0 and A:
                self.st(self.taps["tap_yf"][sl * 128:(sl + 1) * 128, :],
                        self.cvb("yfm", 2048),
                        tuple(f"yfm:{h}" for h in range(H)), tag=f"tapyf{sl}")
            for ti in range(TPS):
                self.oproj_tile(phase, sl, ti, yfm, wT)

    def qkv_tile(self, phase, sl, ti, qfm, cpt, spt):
        S = self.S
        A = phase == "A"
        psA, psB, psBb = self.psA, self.psB, self.psBb
        tl = sl * TPS + ti
        slot = tl % 2
        xt = S["xres"][:, tl * D:(tl + 1) * D]
        ltm = self.cvb("ltm", 2048)[:, slot * 1024:slot * 1024 + 1024]
        lfm = self.cvb("lfm", 2048)[:, slot * 1024:slot * 1024 + 1024]
        qtm = self.cvb("qtm", 2048)[:, slot * 1024:slot * 1024 + 1024]
        ktm = self.cvb("ktm", 512)[:, slot * 256:slot * 256 + 256]
        rsA = self.cvf("rsA", 1024)
        rsB = self.cvf("rsB", 1024)
        kl = lambda nm: f"{nm}:{slot}"
        mk, mean, rk, rstd = self.stats(xt, (f"xres:{tl}",), "ln" if A else "rms",
                                        tag=f"{phase}s{tl}")
        if A:
            def eap(eng, x=xt, o=ltm, m=mean, r=rstd):
                return eng.tensor_scalar(out=o, in0=x, scalar1=m, scalar2=r,
                                         op0=ALU.subtract, op1=ALU.mult)
            self.v(eap, r=(f"xres:{tl}",), sr=(mk, rk), w=(kl("ltm"),), tag=f"Aln{tl}")
        else:
            def eap(eng, x=xt, o=ltm, r=rstd):
                return eng.scalar_tensor_tensor(out=o, in0=x, scalar=r, in1=S["w1b"][:],
                                                op0=ALU.mult, op1=ALU.mult)
            self.v(eap, r=(f"xres:{tl}", "w1b"), sr=(rk,), w=(kl("ltm"),), tag=f"Brm{tl}")
        for cg in range(2):
            def etr(eng, cg=cg, i=ltm):
                inst = None
                for c4 in range(4):
                    c = cg * 4 + c4
                    inst = eng.transpose(
                        out=psBb[:, 1024 + c4 * 128:1024 + (c4 + 1) * 128],
                        in_=i[:, c * 128:(c + 1) * 128], identity=S["s_identb"][:])
                return inst
            self.t(etr, r=(kl("ltm"), "identb"), w=("psB:5",), tag=f"{phase}ltr{tl}")

            def ete(eng, cg=cg, o=lfm):
                return eng.tensor_copy(out=o[:, cg * 512:(cg + 1) * 512],
                                       in_=psBb[:, 1024:1536])
            self.v(ete, r=("psB:5",), w=(kl("lfm"),), tag=f"{phase}lte{tl}")
        lf = lambda c: lfm[:, c * 128:(c + 1) * 128]
        if A:
            pairs = [(S["amats"][:, c * 128:c * 128 + 96], lf(c)) for c in range(NCH)]
            self.mmg(psB[0:96, 0:128], pairs, self.AMK[:3] + (kl("lfm"),), ("psB:4",),
                     tag=f"Au{tl}")
            for i in range(3):
                def eue(eng, i=i):
                    return eng.tensor_copy(out=S["usb"][0:RANK, i * 128:(i + 1) * 128],
                                           in_=psB[i * 32:(i + 1) * 32, 0:128])
                self.v(eue, r=("psB:4",), w=(f"usb:{i}",), tag=f"Aue{tl}_{i}")
        wp = S["wproj"][:]
        nout = 1024 if A else 1536
        for g in range(nout // 512):
            pairs = [(lf(c), wp[:, c * 1536 + g * 512: c * 1536 + (g + 1) * 512])
                     for c in range(NCH)]
            rk2 = [kl("lfm"), "wproj"]
            if A:
                pairs.append((S["usb"][0:RANK, 0:128],
                              S["bmats"][0:RANK, g * 512:(g + 1) * 512]))
                rk2 += ["usb:0", "bm_qB"]
            self.mmg(psA[:, g * 512:(g + 1) * 512], pairs, rk2, (f"psA:{g}",),
                     tag=f"{phase}q{tl}g{g}")
        if A:
            pairs = [(lf(c), wp[:, c * 1536 + 1024: c * 1536 + 1280])
                     for c in range(NCH)]
            pairs.append((S["usb"][0:RANK, 128:256], S["bmats"][0:RANK, 1024:1280]))
            self.mmg(psA[:, 1024:1280], pairs, (kl("lfm"), "wproj", "usb:1", "bm_kB"),
                     ("psA:2",), tag=f"Ak{tl}")
            pairs = [(lf(c), wp[:, c * 1536 + 1280: c * 1536 + 1536])
                     for c in range(NCH)]
            pairs.append((S["usb"][0:RANK, 256:384], S["bmats"][0:RANK, 1280:1536]))
            self.mmg(psA[:, 1280:1536], pairs, (kl("lfm"), "wproj", "usb:2", "bm_vB"),
                     ("psA:2",), tag=f"Av{tl}")
        cp4 = self.tab(cpt, tl, 4)
        sp4 = self.tab4(spt, tl, 4)
        for g in range(4):
            def era(eng, g=g):
                return eng.tensor_tensor(
                    out=rsA[:, g * 256:(g + 1) * 256]
                    .rearrange("p (a b) -> p a b", a=4),
                    in0=psA[:, g * 256:(g + 1) * 256]
                    .rearrange("p (a b) -> p a b", a=4),
                    in1=cp4, op=ALU.mult)
            self.v(era, r=("psA:0", "psA:1", cpt), w=("rsA",),
                   tag=f"{phase}ra{tl}g{g}")

            def erb(eng, g=g):
                return eng.tensor_tensor(
                    out=self.shufw(rsB[:, g * 256:(g + 1) * 256], 4),
                    in0=self.g64(psA[:, g * 256:(g + 1) * 256], 4),
                    in1=sp4, op=ALU.mult)
            self.v(erb, r=("psA:0", "psA:1", spt), w=("rsB",),
                   tag=f"{phase}rb{tl}g{g}")
        for a in range(2):
            def eq(eng, a=a, o=qtm):
                op = bass.AP(o.tensor, o.offset + a * 64,
                             [[o.ap[0][0], 128], [128, 8], [1, 64]])
                ia = bass.AP(rsA.tensor, rsA.offset + a * 512,
                             [[rsA.ap[0][0], 128], [64, 8], [1, 64]])
                ib = bass.AP(rsB.tensor, rsB.offset + a * 512,
                             [[rsB.ap[0][0], 128], [64, 8], [1, 64]])
                return eng.tensor_tensor(out=op, in0=ia, in1=ib, op=ALU.add)
            self.v(eq, r=("rsA", "rsB"), w=(kl("qtm"),), tag=f"{phase}rq{tl}a{a}")
        if A:
            def eg(eng, o=qtm):
                return eng.tensor_tensor(out=o, in0=o, in1=S["gainb"][:], op=ALU.mult)
            self.v(eg, r=(kl("qtm"), "gainb"), w=(kl("qtm"),), tag=f"Agn{tl}")
        cpk = self.tab(cpt, tl, KVH)
        spk4 = self.tab4(spt, tl, KVH)

        def eka(eng):
            return eng.tensor_tensor(
                out=rsA[:, 0:256].rearrange("p (a b) -> p a b", a=KVH),
                in0=psA[:, 1024:1280].rearrange("p (a b) -> p a b", a=KVH),
                in1=cpk, op=ALU.mult)
        self.v(eka, r=("psA:2", cpt), w=("rsA",), tag=f"{phase}ka{tl}")

        def ekb(eng):
            return eng.tensor_tensor(out=self.shufw(rsB[:, 0:256], KVH),
                                     in0=self.g64(psA[:, 1024:1280], KVH),
                                     in1=spk4, op=ALU.mult)
        self.v(ekb, r=("psA:2", spt), w=("rsB",), tag=f"{phase}kb{tl}")

        def ek(eng, o=ktm):
            op = bass.AP(o.tensor, o.offset,
                         [[o.ap[0][0], 128], [64, 2], [128, 2], [1, 64]])
            ia = bass.AP(rsA.tensor, rsA.offset,
                         [[rsA.ap[0][0], 128], [128, 2], [64, 2], [1, 64]])
            ib = bass.AP(rsB.tensor, rsB.offset,
                         [[rsB.ap[0][0], 128], [128, 2], [64, 2], [1, 64]])
            return eng.tensor_tensor(out=op, in0=ia, in1=ib, op=ALU.add)
        self.v(ek, r=("rsA", "rsB"), w=(kl("ktm"),), tag=f"{phase}rk{tl}")

        def ev(eng, o=S["vtm"][:, tl * 256:(tl + 1) * 256]):
            return eng.activation(out=o, in_=psA[:, 1280:1536], func=AF.Copy)
        self.a(ev, r=("psA:2",), w=(f"vtm:{tl}",), tag=f"{phase}ve{tl}")
        for cg in range(2):
            def etq(eng, cg=cg, i=qtm):
                inst = None
                for c4 in range(4):
                    c = cg * 4 + c4
                    inst = eng.transpose(
                        out=psBb[:, 1024 + c4 * 128:1024 + (c4 + 1) * 128],
                        in_=i[:, c * 128:(c + 1) * 128], identity=S["s_identb"][:])
                return inst
            self.t(etq, r=(kl("qtm"), "identb"), w=("psB:5",), tag=f"{phase}qtr{tl}")

            def ete2(eng, cg=cg, ti=ti):
                dst = bass.AP(qfm.tensor, qfm.offset + (cg * 4) * 256 + ti * 128,
                              [[qfm.ap[0][0], 128], [256, 4], [1, 128]])
                return eng.tensor_copy(
                    out=dst, in_=psBb[:, 1024:1536].rearrange("p (c f) -> p c f", c=4))
            self.v(ete2, r=("psB:5",), w=(f"qfm:{ti}:{cg}",), tag=f"{phase}qte{tl}")
        def etk(eng, i=ktm):
            inst = None
            for c in range(2):
                inst = eng.transpose(out=psBb[:, 1024 + c * 128:1024 + (c + 1) * 128],
                                     in_=i[:, c * 128:(c + 1) * 128],
                                     identity=S["s_identb"][:])
            return inst
        self.t(etk, r=(kl("ktm"), "identb"), w=("psB:5",), tag=f"{phase}ktr{tl}")

        def etke(eng, tl=tl):
            kf = S["kfm"][:]
            dst = bass.AP(kf.tensor, kf.offset + tl * 128,
                          [[kf.ap[0][0], 128], [T, 2], [1, 128]])
            return eng.tensor_copy(
                out=dst, in_=psBb[:, 1024:1280].rearrange("p (c f) -> p c f", c=2))
        self.v(etke, r=("psB:5",), w=(f"kfm:{tl}",), tag=f"{phase}kte{tl}")

    def attn_head(self, phase, sl, h, qfm, yfm, Pb, Ptb):
        S = self.S
        A = phase == "A"
        psA, psB, psBb = self.psA, self.psB, self.psBb
        kv = h // (H // KVH)
        nk = (sl + 1) * TPS
        kfmk = tuple(f"kfm:{i}" for i in range(nk))
        for ti in range(TPS):
            tl = sl * TPS + ti
            w = (tl + 1) * 128
            nb = (w + 511) // 512
            skeys = tuple(f"psA:{i}" for i in range(nb))
            qh = qfm[(h // 8) * 64:(h // 8) * 64 + 64,
                     (h % 8) * 256 + ti * 128:(h % 8) * 256 + ti * 128 + 128]
            for b0 in range(0, w, 512):
                b1 = min(w, b0 + 512)

                def esc(eng, b0=b0, b1=b1, qh=qh):
                    kf = S["kfm"]
                    kh = kf[(kv // 2) * 64:(kv // 2) * 64 + 64,
                            (kv % 2) * T + b0:(kv % 2) * T + b1]
                    return eng.matmul(out=psA[:, b0:b1], lhsT=qh, rhs=kh,
                                      start=True, stop=True)
                self.t(esc, r=(f"qfm:{ti}:0", f"qfm:{ti}:1") + kfmk,
                       w=tuple(f"psA:{bb}" for bb in
                               range(b0 // 512, (b1 + 511) // 512)),
                       tag=f"{phase}S{h}_{tl}")

            def emk(eng, w=w):
                return eng.tensor_tensor(out=psA[:, w - 128:w], in0=psA[:, w - 128:w],
                                         in1=S["s_maskadd"][:], op=ALU.add)
            self.v(emk, r=skeys + ("maskadd",), w=(f"psA:{(w - 1) // 512}",),
                   tag=f"{phase}mk{h}_{tl}")
            c = self.sslot(4)
            m, bias = self.sap(c), self.sap(c, 1)
            ssum, rg = self.sap(c, 2), self.sap(c, 3)

            def emx(eng, w=w, o=m):
                return eng.reduce_max(out=o, in_=psA[:, 0:w], axis=AX.X)
            self.v(emx, r=skeys, w=(self.skey(c),), tag=f"{phase}mx{h}_{tl}")

            def ebi(eng, o=bias):
                return eng.tensor_single_scalar(out=o, in_=m, scalar=-SOFT_SCALE,
                                                op=ALU.mult)
            self.v(ebi, sr=(self.skey(c),), w=(self.skey(c, 1),), tag=f"{phase}bi{h}_{tl}")

            def eex(eng, w=w, b=bias, s=ssum):
                return eng.activation(out=Pb[:, 0:w], in_=psA[:, 0:w], func=AF.Exp,
                                      scale=SOFT_SCALE, bias=b, accum_out=s)
            self.a(eex, r=skeys, sr=(self.skey(c, 1),), w=("Pb", self.skey(c, 2)),
                   tag=f"{phase}ex{h}_{tl}")

            def erc(eng, o=rg, s=ssum):
                return eng.reciprocal(out=o, in_=s)
            self.v(erc, sr=(self.skey(c, 2),), w=(self.skey(c, 3),),
                   tag=f"{phase}rc{h}_{tl}")
            if A:
                def egg(eng, o=rg, h=h):
                    return eng.tensor_tensor(out=o, in0=o, in1=S["gateb"][:, h:h + 1],
                                             op=ALU.mult)
                self.v(egg, sr=(self.skey(c, 3),), r=("gateb",), w=(self.skey(c, 3),),
                       tag=f"Agg{h}_{tl}")

            def enr(eng, w=w, rr=rg):
                return eng.tensor_single_scalar(out=Pb[:, 0:w], in_=Pb[:, 0:w],
                                                scalar=rr, op=ALU.mult)
            self.v(enr, r=("Pb",), sr=(self.skey(c, 3),), w=("Pb",),
                   tag=f"{phase}nr{h}_{tl}")
            if ("tap_Pb" in self.taps and A and sl == 4 and h == 6 and ti == 0
                    and self._sq == 0):
                self.st(self.taps["tap_Pb"], Pb[:, 0:1280], ("Pb",), tag="tapPb")
            nch = w // 128
            for cg in range((nch + 3) // 4):
                c0, c1 = cg * 4, min(nch, cg * 4 + 4)

                def etp(eng, c0=c0, c1=c1):
                    inst = None
                    for cc in range(c0, c1):
                        inst = eng.transpose(
                            out=psBb[:, 1024 + (cc - c0) * 128:1024 + (cc - c0 + 1) * 128],
                            in_=Pb[:, cc * 128:(cc + 1) * 128], identity=S["s_identb"][:])
                    return inst
                self.t(etp, r=("Pb", "identb"), w=("psB:5",), tag=f"{phase}Pt{h}_{tl}")

                def etpe(eng, c0=c0, c1=c1, ti=ti):
                    dst = bass.AP(Ptb.tensor, Ptb.offset + c0 * 256 + ti * 128,
                                  [[Ptb.ap[0][0], 128], [256, c1 - c0], [1, 128]])
                    return eng.tensor_copy(
                        out=dst, in_=psBb[:, 1024:1024 + (c1 - c0) * 128]
                        .rearrange("p (c f) -> p c f", c=c1 - c0))
                self.v(etpe, r=("psB:5",),
                       w=tuple(f"Pt:{cc}" for cc in range(c0, c1)),
                       tag=f"{phase}Pe{h}_{tl}")
        if ("tap_Pt" in self.taps and A and sl == 4 and h == 6 and self._sq == 0):
            self.st(self.taps["tap_Pt"], Ptb[:, 0:nk * 256],
                    tuple(f"Pt:{cc}" for cc in range(nk)), tag="tapPt")
        pairs = []
        rks = []
        for cc in range(nk):
            vt = S["vtm"][:, cc * 256 + kv * 64: cc * 256 + kv * 64 + 64]
            pairs.append((vt, Ptb[:, cc * 256:(cc + 1) * 256]))
            rks += [f"vtm:{cc}", f"Pt:{cc}"]
        self.mmg(psB[0:64, 1536:1792], pairs, rks, ("psB:7",), tag=f"{phase}PV{h}_{sl}")

        def eye(eng, h=h):
            dst = yfm[(h % 2) * 64:(h % 2) * 64 + 64,
                      (h // 2) * 256:(h // 2) * 256 + 256]
            return eng.tensor_copy(out=dst, in_=psB[0:64, 1536:1792])
        self.v(eye, r=("psB:7",), w=(f"yfm:{h}",), tag=f"{phase}ye{h}_{sl}")

    def oproj_tile(self, phase, sl, ti, yfm, wT):
        S = self.S
        A = phase == "A"
        psA, psB = self.psA, self.psB
        tl = sl * TPS + ti
        ykeys = tuple(f"yfm:{h}" for h in range(H))
        yf = lambda c: yfm[:, c * 256 + ti * 128: c * 256 + ti * 128 + 128]
        if A:
            pairs = [(S["amats"][:, c * 128 + 96:c * 128 + 128], yf(c))
                     for c in range(NCH)]
            self.mmg(psB[0:32, 0:128], pairs, ("am_pA",) + ykeys, ("psB:4",),
                     tag=f"Aup{tl}")

            def eue(eng):
                return eng.tensor_copy(out=S["usb"][0:RANK, 0:128], in_=psB[0:32, 0:128])
            self.v(eue, r=("psB:4",), w=("usb:0",), tag=f"Aupe{tl}")
        for g in range(2):
            pairs = [(yf(c), wT[:, c * 1024 + g * 512: c * 1024 + (g + 1) * 512])
                     for c in range(NCH)]
            rk2 = list(ykeys) + ["WT"]
            if A:
                pairs.append((S["usb"][0:RANK, 0:128],
                              S["bmats"][0:RANK, 1536 + g * 512:1536 + (g + 1) * 512]))
                rk2 += ["usb:0", "bm_pB"]
            self.mmg(psA[:, g * 512:(g + 1) * 512], pairs, rk2, (f"psA:{g}",),
                     tag=f"{phase}o{tl}g{g}")
        xt = S["xres"][:, tl * D:(tl + 1) * D]
        if A:
            mk, mean, rk, rstd = self.stats(psA[:, 0:1024], ("psA:0", "psA:1"), "ln",
                                            tag=f"Ao{tl}")
            rsA = self.cvf("rsA", 1024)

            def efin(eng, m=mean, r=rstd):
                return eng.tensor_scalar(out=rsA, in0=psA[:, 0:1024], scalar1=m,
                                         scalar2=r, op0=ALU.subtract, op1=ALU.mult)
            self.v(efin, r=("psA:0", "psA:1"), sr=(mk, rk), w=("rsA",), tag=f"Aof{tl}")

            def eadd(eng, x=xt):
                return eng.tensor_tensor(out=x, in0=x, in1=rsA, op=ALU.add)
            self.v(eadd, r=(f"xres:{tl}", "rsA"), w=(f"xres:{tl}",), tag=f"Aoa{tl}")
        else:
            def eadd(eng, x=xt):
                return eng.tensor_tensor(out=x, in0=x, in1=psA[:, 0:1024], op=ALU.add)
            self.v(eadd, r=(f"xres:{tl}", "psA:0", "psA:1"), w=(f"xres:{tl}",),
                   tag=f"Boa{tl}")

    def router_and_h2(self, sq):
        S = self.S
        psA, psB = self.psA, self.psB
        hfm = self.mvb("hfm", NCH * T)
        h2f = self.mvf("h2f", 1024)
        htm = self.mvb("htm", 1024)
        hfmf = self.mvf("hfmf", 1024)
        for tl in range(NT):
            xt = S["xres"][:, tl * D:(tl + 1) * D]
            mk, mean, rk, rstd = self.stats(xt, (f"xres:{tl}",), "rms", tag=f"h2{tl}")

            def ehf(eng, x=xt, r=rstd):
                return eng.scalar_tensor_tensor(out=h2f, in0=x, scalar=r,
                                                in1=S["w2b"][:], op0=ALU.mult,
                                                op1=ALU.mult)
            self.v(ehf, r=(f"xres:{tl}", "w2b"), sr=(rk,), w=("h2f",), tag=f"h2f{tl}")

            def ehb(eng):
                return eng.tensor_copy(out=htm, in_=h2f)
            self.v(ehb, r=("h2f",), w=("htm",), tag=f"h2b{tl}")
            for cg in range(2):
                def etr(eng, cg=cg):
                    inst = None
                    for c4 in range(4):
                        c = cg * 4 + c4
                        inst = eng.transpose(
                            out=self.psBb[:, 1024 + c4 * 128:1024 + (c4 + 1) * 128],
                            in_=htm[:, c * 128:(c + 1) * 128], identity=S["s_identb"][:])
                    return inst
                self.t(etr, r=("htm", "identb"), w=("psB:5",), tag=f"h2tr{tl}")

                def ete(eng, cg=cg, tl=tl):
                    dst = bass.AP(hfm.tensor, hfm.offset + (cg * 4) * T + tl * 128,
                                  [[hfm.ap[0][0], 128], [T, 4], [1, 128]])
                    return eng.tensor_copy(
                        out=dst,
                        in_=self.psBb[:, 1024:1536].rearrange("p (c f) -> p c f", c=4))
                self.v(ete, r=("psB:5",), w=(f"hfm:{tl}:{cg}",), tag=f"h2te{tl}")
            for cg in range(2):
                def etf(eng, cg=cg):
                    inst = None
                    for c4 in range(4):
                        c = cg * 4 + c4
                        inst = eng.transpose(out=psB[:, c4 * 128:(c4 + 1) * 128],
                                             in_=h2f[:, c * 128:(c + 1) * 128],
                                             identity=S["s_identf"][:])
                    return inst
                self.t(etf, r=("h2f", "identf"), w=("psB:4",), tag=f"rtt{tl}")

                def etfe2(eng, cg=cg):
                    return eng.tensor_copy(out=hfmf[:, cg * 512:(cg + 1) * 512],
                                           in_=psB[:, 0:512])
                self.v(etfe2, r=("psB:4",), w=("hfmf",), tag=f"rte{tl}")
            pairs = [(hfmf[:, c * 128:(c + 1) * 128],
                      S["gatef"][:, c * NE:(c + 1) * NE]) for c in range(NCH)]
            self.mmg(psB[:, 1536:1536 + NE], pairs, ("hfmf", "gatef"), ("psB:7",),
                     tag=f"rl{tl}")
            c = self.sslot(8)
            lg = S["scal"][:, c:c + 8]
            lgk = tuple(self.skey(c, i) for i in range(8))

            def ele(eng, lg=lg):
                return eng.tensor_copy(out=lg, in_=psB[:, 1536:1536 + NE])
            self.v(ele, r=("psB:7",), w=lgk, tag=f"rle{tl}")
            self.router_tile(tl, lg, lgk)

    def router_tile(self, tl, lg, lgk):
        S = self.S
        c = self.sslot(6)
        m1, m2 = self.sap(c), self.sap(c, 1)
        den, rden = self.sap(c, 2), self.sap(c, 3)
        m1n = self.sap(c, 4)
        p8 = S["wfull"][:, tl * NE:(tl + 1) * NE]
        tmp8 = S["bn"][:, 24:32]

        def em1(eng):
            return eng.reduce_max(out=m1, in_=lg, axis=AX.X)
        self.v(em1, sr=lgk, w=(self.skey(c),), tag=f"rm1{tl}")

        def emn(eng):
            return eng.tensor_single_scalar(out=m1n, in_=m1, scalar=-1.0, op=ALU.mult)
        self.v(emn, sr=(self.skey(c),), w=(self.skey(c, 4),), tag=f"rmn{tl}")

        def epx(eng):
            return eng.activation(out=p8, in_=lg, func=AF.Exp, scale=1.0, bias=m1n)
        self.a(epx, r=lgk, sr=(self.skey(c, 4),), w=(f"wfull:{tl}",), tag=f"rpx{tl}")

        def eiq(eng):
            return eng.tensor_single_scalar(out=tmp8, in_=lg, scalar=m1, op=ALU.is_equal)
        self.v(eiq, r=lgk, sr=(self.skey(c),), w=("rt8",), tag=f"riq{tl}")

        def emsk(eng):
            return eng.scalar_tensor_tensor(out=tmp8, in0=tmp8, scalar=-65504.0, in1=lg,
                                            op0=ALU.mult, op1=ALU.add)
        self.v(emsk, r=("rt8",) + lgk, w=("rt8",), tag=f"rms{tl}")

        def em2(eng):
            return eng.reduce_max(out=m2, in_=tmp8, axis=AX.X)
        self.v(em2, r=("rt8",), w=(self.skey(c, 1),), tag=f"rm2{tl}")

        def ekp(eng):
            return eng.tensor_single_scalar(out=tmp8, in_=lg, scalar=m2, op=ALU.is_ge)
        self.v(ekp, r=lgk, sr=(self.skey(c, 1),), w=("rt8",), tag=f"rkp{tl}")

        def ewr(eng):
            return eng.tensor_tensor(out=p8, in0=p8, in1=tmp8, op=ALU.mult)
        self.v(ewr, r=(f"wfull:{tl}", "rt8"), w=(f"wfull:{tl}",), tag=f"rwr{tl}")

        def ews(eng):
            return eng.reduce_sum(out=den, in_=p8, axis=AX.X)
        self.v(ews, r=(f"wfull:{tl}",), w=(self.skey(c, 2),), tag=f"rws{tl}")

        def erd(eng):
            return eng.reciprocal(out=rden, in_=den)
        self.v(erd, sr=(self.skey(c, 2),), w=(self.skey(c, 3),), tag=f"rrd{tl}")

        def ewf(eng):
            return eng.tensor_single_scalar(out=p8, in_=p8, scalar=rden, op=ALU.mult)
        self.v(ewf, r=(f"wfull:{tl}",), sr=(self.skey(c, 3),), w=(f"wfull:{tl}",),
               tag=f"rwf{tl}")

    def moe(self, sq):
        S, dram = self.S, self.dram
        psA, psB = self.psA, self.psB
        hfm = self.mvb("hfm", NCH * T)
        gbuf = self.mvb("gbuf", 8192)
        hkeys = tuple(f"hfm:{tl}:{cg}" for tl in range(NT) for cg in range(2))
        wp = S["wproj"][:]
        for e in range(NE):
            for jg in range(4):
                for js in range(4):
                    j = jg * 4 + js
                    for u in range(2):
                        dstu = bass.AP(wp.tensor, wp.offset + js * 256 + u * 128,
                                       [[wp.ap[0][0], 128], [1024, NCH], [1, 128]])
                        srcu = dram["w13T"].rearrange("e (c p) f -> e p c f", p=128)[
                            e, :, :, u * FFN + j * 128: u * FFN + (j + 1) * 128]
                        self.ld(dstu, srcu, (f"w13s:{js}",), rkeys=(f"w13s:{js}",),
                                q="gpsimd", tag=f"mw13_{e}_{j}_{u}")
                    dst2 = wp[:, 8192 + js * 1024:8192 + (js + 1) * 1024]
                    self.ld(dst2, dram["w2T"][e, j * 128:(j + 1) * 128, :],
                            (f"w2s:{js}",), rkeys=(f"w2s:{js}",),
                            q="gpsimd", tag=f"mw2_{e}_{j}")
                    for blk in range(4):
                        pairs = [(wp[:, c * 1024 + js * 256: c * 1024 + js * 256 + 128],
                                  hfm[:, c * T + blk * 512: c * T + (blk + 1) * 512])
                                 for c in range(NCH)]
                        self.mmg(psA[:, blk * 512:(blk + 1) * 512], pairs,
                                 [f"w13s:{js}"] + list(hkeys),
                                 (f"psA:{blk}",), tag=f"m1_{e}_{j}_{blk}")
                    gsl = gbuf[:, js * 2048:(js + 1) * 2048]

                    def esl(eng, o=gsl):
                        return eng.activation(out=o, in_=psA[:, 0:2048], func=AF.Silu)
                    self.a(esl, r=("psA:0", "psA:1", "psA:2", "psA:3"),
                           w=(f"g:{js}",), tag=f"msl{e}_{j}")
                    for blk in range(4):
                        pairs = [(wp[:, c * 1024 + js * 256 + 128:
                                     c * 1024 + js * 256 + 256],
                                  hfm[:, c * T + blk * 512: c * T + (blk + 1) * 512])
                                 for c in range(NCH)]
                        self.mmg(psA[:, blk * 512:(blk + 1) * 512], pairs,
                                 [f"w13s:{js}", f"g:{js}"] + list(hkeys),
                                 (f"psA:{blk}",), tag=f"m3_{e}_{j}_{blk}")

                    def emu(eng, o=gsl):
                        return eng.tensor_tensor(out=o, in0=o, in1=psA[:, 0:2048],
                                                 op=ALU.mult)
                    self.v(emu, r=(f"g:{js}", "psA:0", "psA:1", "psA:2", "psA:3"),
                           w=(f"g:{js}",), tag=f"mmu{e}_{j}")
                for tl in range(NT):
                    for g in range(2):
                        pairs = []
                        for js in range(4):
                            gsl = gbuf[:, js * 2048:(js + 1) * 2048]
                            pairs.append(
                                (gsl[:, tl * 128:(tl + 1) * 128],
                                 wp[:, 8192 + js * 1024 + g * 512:
                                    8192 + js * 1024 + (g + 1) * 512]))
                        self.mmg(psB[:, g * 512:(g + 1) * 512], pairs,
                                 [f"g:{js2}" for js2 in range(4)] +
                                 [f"w2s:{js2}" for js2 in range(4)],
                                 (f"psB:{4 + g}",), tag=f"md{e}_{jg}_{tl}_{g}")
                    xt = S["xres"][:, tl * D:(tl + 1) * D]

                    def eacc(eng, x=xt, e=e, tl=tl):
                        return eng.scalar_tensor_tensor(
                            out=x, in0=psB[:, 0:1024],
                            scalar=S["wfull"][:, tl * NE + e:tl * NE + e + 1],
                            in1=x, op0=ALU.mult, op1=ALU.add)
                    self.v(eacc, r=("psB:4", "psB:5", f"xres:{tl}", f"wfull:{tl}"),
                           w=(f"xres:{tl}",), tag=f"ma{e}_{jg}_{tl}")

    def final_out(self, sq):
        S, dram = self.S, self.dram
        x1t = self.mvf("x1t", 1024)
        res = self.mvf("res", 1024)
        for tl in range(NT):
            self.ld(x1t, dram["x1scr"][tl * 128:(tl + 1) * 128, :],
                    ("x1t",), rkeys=("x1t", f"x1scr:{tl // TPS}"), tag=f"fx1{tl}")
            xt = S["xres"][:, tl * D:(tl + 1) * D]
            mk, mean, rk, rstd = self.stats(xt, (f"xres:{tl}",), "ln", tag=f"fo{tl}")

            def efin(eng, x=xt, m=mean, r=rstd):
                return eng.tensor_scalar(out=res, in0=x, scalar1=m, scalar2=r,
                                         op0=ALU.subtract, op1=ALU.mult)
            self.v(efin, r=(f"xres:{tl}",), sr=(mk, rk), w=("res",), tag=f"ff{tl}")

            def eadd(eng):
                return eng.tensor_tensor(out=res, in0=res, in1=x1t, op=ALU.add)
            self.v(eadd, r=("res", "x1t"), w=("res",), tag=f"fa{tl}")
            self.st(dram["y"][sq * T + tl * 128: sq * T + (tl + 1) * 128, :],
                    res, ("res",), wkeys=("res",), tag=f"fy{sq}_{tl}")


def _host_tables():
    inv = 1.0 / 10000.0 ** (np.arange(0, HD, 2, dtype=np.float32) / HD)
    fr = np.arange(T, dtype=np.float32)[:, None] * inv[None, :]
    cA, sA = np.cos(fr).astype(np.float32), np.sin(fr).astype(np.float32)
    cpA = np.concatenate([cA, cA], axis=1)
    spA = np.concatenate([-sA, sA], axis=1)
    invb = (np.float32(1.0) /
            np.float32(10000.0) ** (np.arange(0, HD, 2, dtype=np.float32) /
                                    np.float32(HD))).astype(np.float32)
    frb = (np.arange(T, dtype=np.float32)[:, None] * invb[None, :]).astype(np.float32)
    cB, sB = np.cos(frb).astype(np.float32), np.sin(frb).astype(np.float32)
    cpB = np.concatenate([cB, cB], axis=1)
    spB = np.concatenate([sB, -sB], axis=1)
    return cpA, spA, cpB, spB


_CACHE = {}


def make_common(inputs):
    import ml_dtypes
    f = lambda nm: np.ascontiguousarray(np.asarray(inputs[nm], dtype=np.float32))
    cpA, spA, cpB, spB = _host_tables()
    row, col = np.arange(128)[:, None], np.arange(128)[None, :]
    maskadd = np.where(col > row, np.float32(MASKVAL), np.float32(0.0)).astype(np.float32)
    common = {
        "Wq": f("L0_attn_Wq"), "Wk": f("L0_attn_Wk"), "Wv": f("L0_attn_Wv"),
        "Wo": f("L0_attn_Wo"),
        "qA": f("L0_attn_qA"), "kA": f("L0_attn_kA"), "vA": f("L0_attn_vA"),
        "pA": f("L0_attn_pA"),
        "qB": f("L0_attn_qB"), "kB": f("L0_attn_kB"), "vB": f("L0_attn_vB"),
        "pB": f("L0_attn_pB"),
        "q_gain": np.repeat(f("L0_attn_q_gain").reshape(H)[
            [(m % 2) * 8 + m // 2 for m in range(H)]], HD).reshape(1, D).copy(),
        "head_gate": f("L0_attn_head_gate").reshape(1, H),
        "ln1_w": f("L0_ffn_ln1_w").reshape(1, D),
        "ln2_w": f("L0_ffn_ln2_w").reshape(1, D),
        "wqkvT": np.ascontiguousarray(f("L0_ffn_wqkv").T),
        "wo2T": np.ascontiguousarray(f("L0_ffn_wo").T),
        "gateT": np.ascontiguousarray(f("L0_ffn_gate_w").T),
        "w13T": np.ascontiguousarray(f("L0_ffn_w13").transpose(0, 2, 1)),
        "w2T": np.ascontiguousarray(f("L0_ffn_w2").transpose(0, 2, 1)),
        "cpA": cpA, "spA": spA, "cpB": cpB, "spB": spB,
        "maskadd": maskadd,
        "identf": np.eye(128, dtype=np.float32),
        "identb": np.eye(128, dtype=np.float32).astype(ml_dtypes.bfloat16),
        "onesrow": np.ones((1, 128), dtype=np.float32),
    }
    return common


def kernel(**inputs):
    if "nc" not in _CACHE:
        kb = KB()
        _CACHE["nc"] = kb.build()
        _CACHE["kb"] = kb
    nc = _CACHE["nc"]
    x = np.ascontiguousarray(np.asarray(inputs["L0_attn_x"], dtype=np.float32))
    common = make_common(inputs)
    in_maps = []
    for c in range(8):
        m = dict(common)
        m["x"] = np.ascontiguousarray(x[c * NSEQ:(c + 1) * NSEQ].reshape(NSEQ * T, D))
        in_maps.append(m)
    res = run_bass_kernel_spmd(nc, in_maps, core_ids=list(range(8)))
    out = np.empty((B, T, D), dtype=np.float32)
    for c in range(8):
        out[c * NSEQ:(c + 1) * NSEQ] = np.asarray(res.results[c]["y"]).reshape(NSEQ, T, D)
    return out
